# version 14
# baseline (speedup 1.0000x reference)
import numpy as np
from contextlib import ExitStack

import concourse.bass as bass
import concourse.mybir as mybir
from concourse.bass_utils import run_bass_kernel_spmd
import ml_dtypes

BF16NP = ml_dtypes.bfloat16
F32 = mybir.dt.float32
BF = mybir.dt.bfloat16
I32 = mybir.dt.int32
AF = mybir.ActivationFunctionType
ALU = mybir.AluOpType

B, T, C = 4, 4096, 2048
H, D, R = 16, 128, 64
HID = 5632
TLOC = 2048
CH = 512
NCT = C // 128
NHT = HID // 128
SCALE_L = 1.0 / float(np.sqrt(D))
SCALE_G = SCALE_L * D / R
SHIFT_L = -24.0
SHIFT_G = -4.0
EPS = 1e-5
PAIRSWAP = [i ^ 1 for i in range(32)]


class Sem:
    def __init__(self, es, nc, name):
        self.h = es.enter_context(nc.semaphore(name))
        self.n = 0


class Defer:

    def __init__(self, block):
        self.block = block
        self.q = {"y": [], "t": [], "s": [], "v": [], "g": []}

    def y(self, f):
        self.q["y"].append(f)

    def t(self, f):
        self.q["t"].append(f)

    def s(self, f):
        self.q["s"].append(f)

    def v(self, f):
        self.q["v"].append(f)

    def g(self, f):
        self.q["g"].append(f)

    def flush(self):
        for f in self.q["y"]:
            self.block.sync(f)
        for f in self.q["t"]:
            self.block.tensor(f)
        for f in self.q["s"]:
            self.block.scalar(f)
        for f in self.q["v"]:
            self.block.vector(f)
        for f in self.q["g"]:
            self.block.gpsimd(f)
        for k in self.q:
            self.q[k] = []


_WAITED = {}


def _w(eng, sem, cnt):
    assert cnt <= sem.n, f"wait on future sem count {cnt} > {sem.n}"
    if cnt > 0:
        key = (getattr(eng, "engine", None) or id(eng), id(sem))
        if _WAITED.get(key, -1) >= cnt:
            return
        _WAITED[key] = cnt
        eng.wait_ge(sem.h, cnt)


def _ln_stats(v, xsrc, st, tmp, vc):
    for g in range(3):
        v.bn_stats(st[:, g * 6:(g + 1) * 6], xsrc[:, g * 512:(g + 1) * 512])
    vc(v.bn_stats(st[:, 18:24], xsrc[:, 1536:2048]))
    stv = st.ap().rearrange("p (g three) -> p g three", three=3)
    means = stv[:, :, 1]
    cvs = stv[:, :, 2]
    t = tmp
    v.reduce_sum(t[:, 0:1], means, axis=mybir.AxisListType.X)
    v.tensor_tensor(t[:, 8:16], means, means, ALU.mult)
    vc(v.reduce_sum(t[:, 4:5], cvs, axis=mybir.AxisListType.X))
    v.tensor_scalar_mul(t[:, 2:3], t[:, 0:1], 0.125)
    v.reduce_sum(t[:, 3:4], t[:, 8:16], axis=mybir.AxisListType.X)
    vc(v.tensor_scalar_mul(t[:, 4:5], t[:, 4:5], 1.0 / 2048.0))
    v.scalar_tensor_tensor(t[:, 5:6], t[:, 3:4], 0.125, t[:, 4:5], ALU.mult, ALU.add)
    vc(v.tensor_mul(t[:, 6:7], t[:, 2:3], t[:, 2:3]))
    return v.tensor_sub(t[:, 5:6], t[:, 5:6], t[:, 6:7])


def build(debug=False, phases=5):
    _WAITED.clear()
    nc = bass.Bass("TRN2", target_bir_lowering=False)
    es = ExitStack()

    dx = nc.dram_tensor("x_own", [TLOC, C], F32, kind="ExternalInput").ap()
    dxp = nc.dram_tensor("x_par", [TLOC, C], F32, kind="ExternalInput").ap()
    dwqkvT = nc.dram_tensor("wqkvT", [C, 3 * C], F32, kind="ExternalInput").ap()
    dwoT = nc.dram_tensor("woT", [C, C], F32, kind="ExternalInput").ap()
    dpkT = nc.dram_tensor("pkT", [D, R], F32, kind="ExternalInput").ap()
    dpvT = nc.dram_tensor("pvT", [D, R], F32, kind="ExternalInput").ap()
    duoT = nc.dram_tensor("uoT", [R, D], F32, kind="ExternalInput").ap()
    dw1T = nc.dram_tensor("w1T", [C, 512], F32, kind="ExternalInput").ap()
    dw2T = nc.dram_tensor("w2T", [128, 4], F32, kind="ExternalInput").ap()
    dwgT = nc.dram_tensor("wgT", [C, HID], I32, kind="ExternalInput").ap()
    dwuT = nc.dram_tensor("wuT", [C, HID], I32, kind="ExternalInput").ap()
    dwdT = nc.dram_tensor("wdT", [HID, C], I32, kind="ExternalInput").ap()
    dsg = nc.dram_tensor("sg2", [128, NHT], F32, kind="ExternalInput").ap()
    dsu = nc.dram_tensor("su2", [128, NHT], F32, kind="ExternalInput").ap()
    dsd = nc.dram_tensor("sd2", [128, NCT], F32, kind="ExternalInput").ap()
    dropeC = nc.dram_tensor("ropeC", [D, 4096], F32, kind="ExternalInput").ap()
    dropeS = nc.dram_tensor("ropeS", [D, 4096], F32, kind="ExternalInput").ap()
    dident = nc.dram_tensor("identB", [128, 128], BF, kind="ExternalInput").ap()
    didentF = nc.dram_tensor("identF", [128, 128], F32, kind="ExternalInput").ap()
    dmasklo = nc.dram_tensor("masklo", [128, 128], BF, kind="ExternalInput").ap()
    dmaskhi = nc.dram_tensor("maskhi", [128, 128], BF, kind="ExternalInput").ap()
    dbmask = nc.dram_tensor("bmask", [4, 128, 512], BF, kind="ExternalInput").ap()
    dcst = nc.dram_tensor("cst", [128, 8], F32, kind="ExternalInput").ap()
    donesrow = nc.dram_tensor("onesrow", [1, 128], BF, kind="ExternalInput").ap()

    dout = nc.dram_tensor("out", [TLOC, C], F32, kind="ExternalOutput").ap()
    scr_q = nc.dram_tensor("scr_q", [H, 128, TLOC], BF, kind="Internal").ap()
    scr_k = nc.dram_tensor("scr_k", [H, 128, 4096], BF, kind="Internal").ap()
    scr_vs = nc.dram_tensor("scr_vs", [H, 128, 2560], BF, kind="Internal").ap()
    scr_kr = nc.dram_tensor("scr_kr", [H, 64, 4096], BF, kind="Internal").ap()
    scr_qr = nc.dram_tensor("scr_qr", [H, 64, TLOC], BF, kind="Internal").ap()
    scr_vra = nc.dram_tensor("scr_vra", [H, 128, 32 * 65], BF, kind="Internal").ap()
    scr_x1 = nc.dram_tensor("scr_x1", [TLOC, C], F32, kind="Internal").ap()
    scr_h = nc.dram_tensor("scr_h", [NHT, 128, 1024], BF, kind="Internal").ap()
    scr_d = nc.dram_tensor("scr_d", [NCT, 128, 1024], BF, kind="Internal").ap()
    scr_wgbf = nc.dram_tensor("scr_wgbf", [NHT, 128, NCT * 128], BF, kind="Internal").ap()
    scr_wubf = nc.dram_tensor("scr_wubf", [NHT, 128, NCT * 128], BF, kind="Internal").ap()
    scr_wdbf = nc.dram_tensor("scr_wdbf", [NCT, 128, NHT * 128], BF, kind="Internal").ap()
    scr_wobf = nc.dram_tensor("scr_wobf", [NCT, 128, NCT * 128], BF, kind="Internal").ap()
    scr_xn2 = nc.dram_tensor("scr_xn2", [4, NCT, 128, CH], BF, kind="Internal").ap()

    dbg = {}
    if debug:
        def dbgt(name, shape, dt=BF):
            dbg[name] = nc.dram_tensor("dbg_" + name, shape, dt, kind="ExternalOutput").ap()
        dbgt("xnT", [NCT, 128, 4096])
        dbgt("gate", [1, TLOC], F32)
        dbgt("q", [H, 128, TLOC])
        dbgt("k", [H, 128, 4096])
        dbgt("vs", [H, 128, 2560])
        dbgt("kr", [H, 64, 4096])
        dbgt("qr", [H, 64, TLOC])
        dbgt("vra", [H, 128, 32 * 65])
        dbgt("ctx", [128, H * TLOC])
        dbgt("x1", [TLOC, C], F32)
        dbgt("xn2", [4, NCT, 128, CH])
        dbgt("h", [2, NHT, 128, 1024])
        dbgt("dn", [2, NCT, 128, 1024])

    esx = ExitStack()
    xnT = esx.enter_context(nc.sbuf_tensor("xnT", [128, NCT * 4096], BF))
    gate_row = es.enter_context(nc.sbuf_tensor("gate_row", [1, TLOC], F32, side="right"))
    cst = es.enter_context(nc.sbuf_tensor("cst_sb", [128, 8], F32, side="right"))
    identB = es.enter_context(nc.sbuf_tensor("identB_sb", [128, 128], BF, side="right"))
    identF = es.enter_context(nc.sbuf_tensor("identF_sb", [128, 128], F32, side="right"))
    onescol = es.enter_context(nc.sbuf_tensor("onescol", [128, 1], BF, side="right"))
    onesrow = es.enter_context(nc.sbuf_tensor("onesrow_sb", [1, 128], BF, side="right"))
    masklo = es.enter_context(nc.sbuf_tensor("masklo_sb", [128, 128], BF, side="right"))
    maskhi = es.enter_context(nc.sbuf_tensor("maskhi_sb", [128, 128], BF, side="right"))
    bmask = es.enter_context(nc.sbuf_tensor("bmask_sb", [128, 4 * 512], BF, side="right"))
    uoTb = es.enter_context(nc.sbuf_tensor("uoTb", [64, 128], BF, side="right"))
    pkb = es.enter_context(nc.sbuf_tensor("pkb", [128, R], BF, side="right"))
    pkb2 = es.enter_context(nc.sbuf_tensor("pkb2", [128, R], BF, side="right"))
    pvb = es.enter_context(nc.sbuf_tensor("pvb", [128, R], BF, side="right"))
    ropeC = esx.enter_context(nc.sbuf_tensor("ropeC_sb", [128, 4096], BF))
    ropeS = esx.enter_context(nc.sbuf_tensor("ropeS_sb", [128, 4096], BF))

    s_vc = Sem(es, nc, "s_vc")

    def vchain(inst):
        s_vc.n += 1
        inst.then_inc(s_vc.h, 1)
        nc.vector.wait_ge(s_vc.h, s_vc.n)
        return inst

    s_ld = Sem(es, nc, "s_ld")
    s_st = Sem(es, nc, "s_st")
    s_mm = Sem(es, nc, "s_mm")
    s_dv = Sem(es, nc, "s_dv")
    s_ac = Sem(es, nc, "s_ac")

    CST_SHIFT_L = cst[:, 0:1]
    CST_SHIFT_G = cst[:, 1:2]
    CST_EPS = cst[:, 2:3]

    blk = nc.Block()
    block = blk.__enter__()

    def barrier():
        for engdec in (block.sync, block.vector, block.scalar, block.tensor, block.gpsimd):
            @engdec
            def _(e):
                for s in (s_ld, s_st, s_mm, s_dv, s_ac):
                    if s.n > 0:
                        e.wait_ge(s.h, s.n)

    @block.sync
    def _(sync):
        sync.dma_start(cst[:], dcst[:]).then_inc(s_ld.h, 16)
        sync.dma_start(identB[:], dident[:]).then_inc(s_ld.h, 16)
        sync.dma_start(identF[:], didentF[:]).then_inc(s_ld.h, 16)
        sync.dma_start(masklo[:], dmasklo[:]).then_inc(s_ld.h, 16)
        sync.dma_start(maskhi[:], dmaskhi[:]).then_inc(s_ld.h, 16)
        sync.dma_start(bmask.ap().rearrange("p (j t) -> p j t", j=4),
                       dbmask.rearrange("j p t -> p j t")).then_inc(s_ld.h, 16)
        sync.dma_start(onesrow[:], donesrow[:]).then_inc(s_ld.h, 16)
    s_ld.n += 7 * 16

    @block.vector
    def _(v):
        _w(v, s_ld, s_ld.n)
        v.memset(onescol[:], 1.0).then_inc(s_dv.h, 1)
    s_dv.n += 1

    es0 = ExitStack()
    p0_xbuf = [es0.enter_context(nc.sbuf_tensor(f"p0x{i}", [128, C], F32)) for i in range(2)]
    p0_xn = [es0.enter_context(nc.sbuf_tensor(f"p0n{i}", [128, C], BF)) for i in range(2)]
    p0_st = es0.enter_context(nc.sbuf_tensor("p0st", [128, 24], F32))
    p0_tmp = es0.enter_context(nc.sbuf_tensor("p0tmp", [128, 16], F32))
    es0p = ExitStack()
    p0_ps = [es0p.enter_context(nc.psum_tensor(f"p0ps{i}", [128, 1024], BF)) for i in range(2)]
    p_info = es0p.enter_context(nc.psum_tensor("p_info", [128, 512], F32))
    p_z = es0p.enter_context(nc.psum_tensor("p_z", [128, 512], F32))

    xbuf_free = [0, 0]
    xn_free = [0, 0]
    ps_free = [0, 0]

    for half in range(2):
        src = dx if half == 0 else dxp
        for tt in range(16):
            bi = (half * 16 + tt) % 2
            tglob = half * 2048 + tt * 128

            @block.sync
            def _(sync, bi=bi, tt=tt, src=src):
                _w(sync, s_dv, xbuf_free[bi])
                sync.dma_start(p0_xbuf[bi][:], src[tt * 128:(tt + 1) * 128, :]).then_inc(s_ld.h, 16)
            s_ld.n += 16
            ld_done = s_ld.n

            @block.vector
            def _(v, bi=bi):
                _w(v, s_ld, ld_done)
                _ln_stats(v, p0_xbuf[bi], p0_st, p0_tmp, vchain).then_inc(s_dv.h, 1)
            s_dv.n += 1
            var_done = s_dv.n

            @block.scalar
            def _(s):
                _w(s, s_dv, var_done)
                s.activation(p0_tmp[:, 7:8], p0_tmp[:, 5:6], AF.Sqrt, bias=CST_EPS).then_inc(s_ac.h, 1)
            s_ac.n += 1
            sd_done = s_ac.n

            @block.vector
            def _(v, bi=bi):
                _w(v, s_ac, sd_done)
                _w(v, s_dv, xn_free[bi])
                vchain(v.reciprocal(p0_tmp[:, 6:7], p0_tmp[:, 7:8]))
                v.tensor_scalar(p0_xn[bi][:], p0_xbuf[bi][:], p0_tmp[:, 2:3],
                                p0_tmp[:, 6:7], ALU.subtract, ALU.mult).then_inc(s_dv.h, 1)
            s_dv.n += 1
            xn_done = s_dv.n
            xbuf_free[bi] = xn_done

            @block.tensor
            def _(tns, bi=bi):
                _w(tns, s_dv, xn_done)
                for grp in range(2):
                    _w(tns, s_dv, ps_free[grp])
                    for j in range(8):
                        ct = grp * 8 + j
                        i = tns.transpose(p0_ps[grp][:, j * 128:(j + 1) * 128],
                                          p0_xn[bi][:, ct * 128:(ct + 1) * 128], identB[:])
                    i.then_inc(s_mm.h, 1)
            s_mm.n += 2
            tr_done = s_mm.n

            @block.vector
            def _(v, tglob=tglob):
                _w(v, s_mm, tr_done - 1)
                for j in range(8):
                    ct = j
                    i = v.tensor_copy(xnT[:, ct * 4096 + tglob:ct * 4096 + tglob + 128],
                                      p0_ps[0][:, j * 128:(j + 1) * 128])
                i.then_inc(s_dv.h, 1)
            s_dv.n += 1
            ps_free[0] = s_dv.n

            @block.scalar
            def _(s, tglob=tglob):
                _w(s, s_mm, tr_done)
                for j in range(8):
                    ct = 8 + j
                    i = s.copy(xnT[:, ct * 4096 + tglob:ct * 4096 + tglob + 128],
                               p0_ps[1][:, j * 128:(j + 1) * 128])
                i.then_inc(s_dv.h, 1)
            s_dv.n += 1
            ps_free[1] = s_dv.n
            xn_free[bi] = s_dv.n

    es0i = ExitStack()
    w1b = es0i.enter_context(nc.sbuf_tensor("w1b", [128, NCT * 4 * 128], BF))
    w2b = es0i.enter_context(nc.sbuf_tensor("w2b", [128, 4], BF))
    hinfo = es0i.enter_context(nc.sbuf_tensor("hinfo", [128, 4 * 512], BF))
    sigz = es0i.enter_context(nc.sbuf_tensor("sigz", [1, 512], F32))

    w1cast_prev = 0
    for rt in range(4):
        @block.sync
        def _(sync, rt=rt, wp=w1cast_prev):
            if rt == 0:
                _w(sync, s_dv, max(xbuf_free))
            else:
                _w(sync, s_dv, wp)
            sync.dma_start(p0_xbuf[0].ap().rearrange("p (ct f) -> p ct f", ct=16),
                           dw1T[:, rt * 128:(rt + 1) * 128].rearrange("(ct p) f -> p ct f", p=128)
                           ).then_inc(s_ld.h, 16)
            if rt == 0:
                sync.dma_start(p0_xbuf[1][:, 0:4], dw2T[:]).then_inc(s_ld.h, 16)
        s_ld.n += 32 if rt == 0 else 16
        w1_ld = s_ld.n

        @block.vector
        def _(v, rt=rt):
            _w(v, s_ld, w1_ld)
            i = v.tensor_copy(w1b.ap().rearrange("p (ct rt f) -> p ct rt f", ct=16, rt=4)[:, :, rt, :],
                              p0_xbuf[0].ap().rearrange("p (ct f) -> p ct f", ct=16))
            if rt == 0:
                i = v.tensor_copy(w2b[:], p0_xbuf[1][:, 0:4])
            i.then_inc(s_dv.h, 1)
        s_dv.n += 1
        w1cast_prev = s_dv.n
    infow_cast = s_dv.n

    info_ps_free = 0
    z_ps_free = 0
    hinfo_free = 0
    sigz_read = 0
    for ch in range(4):
        for rt in range(4):
            @block.tensor
            def _(tns, ch=ch, rt=rt, ipf=info_ps_free):
                if ch == 0 and rt == 0:
                    _w(tns, s_dv, infow_cast)
                _w(tns, s_ac, ipf)
                for ct in range(16):
                    i = tns.matmul(p_info[:],
                                   w1b[:, (ct * 4 + rt) * 128:(ct * 4 + rt + 1) * 128],
                                   xnT[:, ct * 4096 + ch * 512:ct * 4096 + (ch + 1) * 512],
                                   start=(ct == 0), stop=(ct == 15))
                i.then_inc(s_mm.h, 1)
            s_mm.n += 1
            mm_done = s_mm.n

            @block.scalar
            def _(s, rt=rt, hf=hinfo_free):
                _w(s, s_mm, max(mm_done, hf))
                s.activation(hinfo[:, rt * 512:(rt + 1) * 512],
                             p_info[:], AF.Gelu).then_inc(s_ac.h, 1)
            s_ac.n += 1
            info_ps_free = s_ac.n
        gelu_done = s_ac.n

        @block.tensor
        def _(tns, ch=ch, zpf=z_ps_free, gd=gelu_done):
            _w(tns, s_ac, max(gd, zpf))
            for rt in range(4):
                i = tns.matmul(p_z[0:1, 0:512], w2b[:, rt:rt + 1],
                               hinfo[:, rt * 512:(rt + 1) * 512],
                               start=(rt == 0), stop=(rt == 3))
            i.then_inc(s_mm.h, 1)
        s_mm.n += 1
        z_done = s_mm.n
        hinfo_free = z_done

        @block.scalar
        def _(s, ch=ch, szr=sigz_read):
            _w(s, s_mm, z_done)
            _w(s, s_dv, szr)
            s.activation(sigz[0:1, 0:512], p_z[0:1, 0:512],
                         AF.Sigmoid).then_inc(s_ac.h, 1)
        s_ac.n += 1
        sig_done = s_ac.n
        z_ps_free = sig_done

        @block.vector
        def _(v, ch=ch):
            _w(v, s_ac, sig_done)
            v.tensor_single_scalar(gate_row[0:1, ch * 512:(ch + 1) * 512],
                                   sigz[0:1, 0:512], 0.75,
                                   ALU.is_gt).then_inc(s_dv.h, 1)
        s_dv.n += 1
        sigz_read = s_dv.n

    @block.sync
    def _(sync):
        _w(sync, s_dv, s_dv.n)
        sync.dma_start(p0_xbuf[0][:], dropeC[:, 0:2048]).then_inc(s_ld.h, 16)
        sync.dma_start(p0_xbuf[1][:], dropeC[:, 2048:4096]).then_inc(s_ld.h, 16)
    s_ld.n += 32
    rc_ld = s_ld.n

    @block.vector
    def _(v):
        _w(v, s_ld, rc_ld)
        v.tensor_copy(ropeC[:, 0:2048], p0_xbuf[0][:])
        v.tensor_copy(ropeC[:, 2048:4096], p0_xbuf[1][:]).then_inc(s_dv.h, 1)
    s_dv.n += 1
    rc_cast = s_dv.n

    @block.sync
    def _(sync):
        _w(sync, s_dv, rc_cast)
        sync.dma_start(p0_xbuf[0][:], dropeS[:, 0:2048]).then_inc(s_ld.h, 16)
        sync.dma_start(p0_xbuf[1][:], dropeS[:, 2048:4096]).then_inc(s_ld.h, 16)
    s_ld.n += 32
    rs_ld = s_ld.n

    @block.vector
    def _(v):
        _w(v, s_ld, rs_ld)
        v.tensor_copy(ropeS[:, 0:2048], p0_xbuf[0][:])
        v.tensor_copy(ropeS[:, 2048:4096], p0_xbuf[1][:]).then_inc(s_dv.h, 1)
    s_dv.n += 1
    rs_cast = s_dv.n

    @block.sync
    def _(sync):
        _w(sync, s_dv, rs_cast)
        sync.dma_start(p0_xbuf[0][:, 0:R], dpkT[:]).then_inc(s_ld.h, 16)
        sync.dma_start(p0_xbuf[0][:, R:2 * R], dpvT[:]).then_inc(s_ld.h, 16)
        sync.dma_start(p0_xbuf[0][0:64, 2 * R:2 * R + 128], duoT[:]).then_inc(s_ld.h, 16)
    s_ld.n += 48
    pk_ld = s_ld.n

    @block.vector
    def _(v):
        _w(v, s_ld, pk_ld)
        v.tensor_copy(pkb[:], p0_xbuf[0][:, 0:R])
        v.tensor_scalar(pkb2[:], p0_xbuf[0][:, 0:R], float(D) / float(R), None, ALU.mult)
        v.tensor_copy(pvb[:], p0_xbuf[0][:, R:2 * R])
        v.tensor_copy(uoTb[:], p0_xbuf[0][0:64, 2 * R:2 * R + 128]).then_inc(s_dv.h, 1)
    s_dv.n += 1

    if debug:
        @block.gpsimd
        def _(g):
            _w(g, s_dv, s_dv.n)
            g.dma_start(dbg["xnT"].rearrange("ct p t -> p ct t"),
                        xnT.ap().rearrange("p (ct t) -> p ct t", ct=NCT)).then_inc(s_st.h, 16)
            g.dma_start(dbg["gate"][:], gate_row[:]).then_inc(s_st.h, 16)
        s_st.n += 32

    barrier()
    es0i.close()
    es0.close()
    es0p.close()

    if phases <= 0:
        return _finish(nc, blk, block, es, (s_ld, s_st, s_mm, s_dv, s_ac)), dbg

    es1 = ExitStack()
    wld = es1.enter_context(nc.sbuf_tensor("p1w", [128, NCT * 128], F32))
    wbf = es1.enter_context(nc.sbuf_tensor("p1wb", [128, NCT * 128], BF))
    stage = [es1.enter_context(nc.sbuf_tensor(f"p1s{i}", [128, 4096], BF)) for i in range(2)]
    rtmp = es1.enter_context(nc.sbuf_tensor("p1rt", [128, 3 * CH], BF))
    rstage = es1.enter_context(nc.sbuf_tensor("p1r", [128, 4096], BF))
    vsst = es1.enter_context(nc.sbuf_tensor("p1vs", [128, 2560], BF))
    es1p = ExitStack()
    p1_pa = [es1p.enter_context(nc.psum_tensor(f"p1pa{i}", [128, 512], F32)) for i in range(2)]
    p1_pr = [es1p.enter_context(nc.psum_tensor(f"p1pr{i}", [128, 512], F32)) for i in range(2)]
    p1_pt = [es1p.enter_context(nc.psum_tensor(f"p1pt{i}", [128, 1024], BF)) for i in range(2)]

    wld_free = 0
    wbf_free = 0
    stage_free = [0, 0]
    rstage_free = 0
    vsst_free = 0
    pa_free = [0, 0]
    pr_free = [0, 0]
    pt_free = [0, 0]

    BAND_COLS = list(range(0, 2048, 128)) + list(range(3584, 4096, 128))

    for ft in range(48):
        fb = ft % 2
        kind = ft // 16
        head = ft % 16
        chunks = range(4) if kind == 0 else range(8)

        @block.sync
        def _(sync, ft=ft, wf=wld_free):
            _w(sync, s_dv, wf)
            sync.dma_start(wld.ap().rearrange("p (ct f) -> p ct f", ct=NCT),
                           dwqkvT[:, ft * 128:(ft + 1) * 128].rearrange("(ct p) f -> p ct f", p=128)
                           ).then_inc(s_ld.h, 16)
        s_ld.n += 16
        w_ld = s_ld.n

        @block.vector
        def _(v, wbff=wbf_free, kind=kind):
            _w(v, s_ld, w_ld)
            _w(v, s_mm, wbff)
            if kind == 0:
                v.tensor_scalar(wbf[:], wld[:], SCALE_L, None, ALU.mult).then_inc(s_dv.h, 1)
            else:
                v.tensor_copy(wbf[:], wld[:]).then_inc(s_dv.h, 1)
        s_dv.n += 1
        w_cast = s_dv.n
        wld_free = w_cast

        for ci, ch in enumerate(chunks):
            pb = ci % 2

            @block.tensor
            def _(tns, fb=fb, ch=ch, pb=pb, ci=ci, paf=pa_free[pb], sf=stage_free[fb]):
                if ci == 0:
                    _w(tns, s_dv, w_cast)
                    _w(tns, s_st, sf)
                _w(tns, s_dv, paf)
                for ct in range(NCT):
                    i = tns.matmul(p1_pa[pb][:], wbf[:, ct * 128:(ct + 1) * 128],
                                   xnT[:, ct * 4096 + ch * 512:ct * 4096 + (ch + 1) * 512],
                                   start=(ct == 0), stop=(ct == NCT - 1))
                i.then_inc(s_mm.h, 1)
            s_mm.n += 1
            mm_done = s_mm.n
            if ci == len(chunks) - 1:
                wbf_free = mm_done

            col0 = ch * 512
            if kind == 2:
                @block.vector
                def _(v, fb=fb, pb=pb, col0=col0):
                    _w(v, s_mm, mm_done)
                    v.tensor_copy(stage[fb][:, col0:col0 + 512], p1_pa[pb][:]).then_inc(s_dv.h, 1)
                s_dv.n += 1
                pa_free[pb] = s_dv.n
            else:
                copy_cnt = s_dv.n + 1
                @block.vector
                def _(v, fb=fb, pb=pb, col0=col0, cc=copy_cnt):
                    _w(v, s_mm, mm_done)
                    st = stage[fb][:, col0:col0 + 512]
                    v.tensor_copy(st, p1_pa[pb][:]).then_inc(s_dv.h, 1)
                    v.wait_ge(s_dv.h, cc)
                    v.stream_shuffle(rtmp[:, 0:512], st, PAIRSWAP)
                    vchain(v.tensor_mul(rtmp[:, 512:1024], st, ropeC[:, col0:col0 + 512]))
                    vchain(v.tensor_mul(rtmp[:, 1024:1536], rtmp[:, 0:512], ropeS[:, col0:col0 + 512]))
                    v.tensor_add(st, rtmp[:, 512:1024], rtmp[:, 1024:1536]).then_inc(s_dv.h, 1)
                s_dv.n += 2
                pa_free[pb] = s_dv.n - 1
        stage_done = s_dv.n

        if kind in (0, 1):
            nj = 4 if kind == 0 else 8
            for j in range(nj):
                @block.tensor
                def _(tns, fb=fb, j=j, prf=pr_free[j % 2], kind=kind):
                    if j == 0:
                        _w(tns, s_dv, stage_done)
                    _w(tns, s_dv, prf)
                    tns.matmul(p1_pr[j % 2][0:64, 0:512],
                               (pkb if kind == 0 else pkb2)[:, 0:64],
                               stage[fb][:, j * 512:(j + 1) * 512],
                               start=True, stop=True).then_inc(s_mm.h, 1)
                s_mm.n += 1
                prmm = s_mm.n

                @block.vector
                def _(v, j=j, prmm=prmm, rf=rstage_free):
                    _w(v, s_mm, prmm)
                    if j == 0:
                        _w(v, s_st, rf)
                    v.tensor_copy(rstage[0:64, j * 512:(j + 1) * 512],
                                  p1_pr[j % 2][0:64, 0:512]).then_inc(s_dv.h, 1)
                s_dv.n += 1
                pr_free[j % 2] = s_dv.n
            proj_done = s_dv.n

            @block.gpsimd
            def _(g, fb=fb, head=head, kind=kind):
                _w(g, s_dv, proj_done)
                if kind == 0:
                    g.dma_start(scr_q[head], stage[fb][:, 0:2048]).then_inc(s_st.h, 16)
                    g.dma_start(scr_qr[head], rstage[0:64, 0:2048]).then_inc(s_st.h, 16)
                    if debug:
                        g.dma_start(dbg["q"][head], stage[fb][:, 0:2048]).then_inc(s_st.h, 16)
                        g.dma_start(dbg["qr"][head], rstage[0:64, 0:2048]).then_inc(s_st.h, 16)
                else:
                    g.dma_start(scr_k[head], stage[fb][:]).then_inc(s_st.h, 16)
                    g.dma_start(scr_kr[head], rstage[0:64, 0:4096]).then_inc(s_st.h, 16)
                    if debug:
                        g.dma_start(dbg["k"][head], stage[fb][:]).then_inc(s_st.h, 16)
                        g.dma_start(dbg["kr"][head], rstage[0:64, 0:4096]).then_inc(s_st.h, 16)
            s_st.n += 64 if debug else 32
            stage_free[fb] = s_st.n
            rstage_free = s_st.n

        else:
            @block.vector
            def _(v, rf=rstage_free):
                _w(v, s_st, rf)
                v.memset(rstage[:], 1.0).then_inc(s_dv.h, 1)
            s_dv.n += 1
            vra_init = s_dv.n

            for si in range(32):
                @block.tensor
                def _(tns, fb=fb, si=si, prf=pr_free[si % 2]):
                    if si == 0:
                        _w(tns, s_dv, stage_done)
                    _w(tns, s_dv, prf)
                    tns.matmul(p1_pr[si % 2][:, 0:64], stage[fb][:, si * 128:(si + 1) * 128],
                               pvb[:, 0:64], start=True, stop=True).then_inc(s_mm.h, 1)
                s_mm.n += 1
                prmm = s_mm.n

                @block.vector
                def _(v, si=si, prmm=prmm):
                    _w(v, s_mm, prmm)
                    if si == 0:
                        _w(v, s_dv, vra_init)
                    v.tensor_copy(rstage[:, si * 65:si * 65 + 64],
                                  p1_pr[si % 2][:, 0:64]).then_inc(s_dv.h, 1)
                s_dv.n += 1
                pr_free[si % 2] = s_dv.n
            proj_done = s_dv.n

            grp_sizes = [8, 8, 4]
            for gi, gs in enumerate(grp_sizes):
                gslot = gi % 2
                @block.tensor
                def _(tns, fb=fb, gi=gi, gs=gs, ptf=pt_free[gslot], gslot=gslot):
                    _w(tns, s_dv, ptf)
                    for j in range(gs):
                        scol = BAND_COLS[gi * 8 + j]
                        i = tns.transpose(p1_pt[gslot][:, j * 128:(j + 1) * 128],
                                          stage[fb][:, scol:scol + 128], identB[:])
                    i.then_inc(s_mm.h, 1)
                s_mm.n += 1
                trv = s_mm.n

                @block.vector
                def _(v, gi=gi, gs=gs, vf=vsst_free, trv=trv, gslot=gslot):
                    _w(v, s_mm, trv)
                    if gi == 0:
                        _w(v, s_st, vf)
                    for j in range(gs):
                        jj = gi * 8 + j
                        i = v.tensor_copy(vsst[:, jj * 128:(jj + 1) * 128],
                                          p1_pt[gslot][:, j * 128:(j + 1) * 128])
                    i.then_inc(s_dv.h, 1)
                s_dv.n += 1
                pt_free[gslot] = s_dv.n
            vs_done = s_dv.n

            @block.gpsimd
            def _(g, fb=fb, head=head):
                _w(g, s_dv, vs_done)
                g.dma_start(scr_vs[head], vsst[:]).then_inc(s_st.h, 16)
                g.dma_start(scr_vra[head], rstage[:, 0:32 * 65]).then_inc(s_st.h, 16)
                if debug:
                    g.dma_start(dbg["vs"][head], vsst[:]).then_inc(s_st.h, 16)
                    g.dma_start(dbg["vra"][head], rstage[:, 0:32 * 65]).then_inc(s_st.h, 16)
            s_st.n += 64 if debug else 32
            stage_free[fb] = s_st.n
            rstage_free = s_st.n
            vsst_free = s_st.n

    barrier()
    es1.close()
    es1p.close()
    esx.close()

    if phases <= 1:
        return _finish(nc, blk, block, es, (s_ld, s_st, s_mm, s_dv, s_ac)), dbg

    es_c = ExitStack()
    ctx_all = es_c.enter_context(nc.sbuf_tensor("ctx_all", [128, H * TLOC], BF))
    es2 = ExitStack()
    qb2 = [es2.enter_context(nc.sbuf_tensor(f"qb{i}", [128, TLOC], BF)) for i in range(2)]
    kb2 = [es2.enter_context(nc.sbuf_tensor(f"kb{i}", [128, 2560], BF)) for i in range(2)]
    vsb2 = [es2.enter_context(nc.sbuf_tensor(f"vsb{i}", [128, 2560], BF)) for i in range(2)]
    krb2 = [es2.enter_context(nc.sbuf_tensor(f"krb{i}", [128, 4096], BF)) for i in range(2)]
    qrb2 = [es2.enter_context(nc.sbuf_tensor(f"qrb{i}", [128, TLOC], BF)) for i in range(2)]
    vrab2 = [es2.enter_context(nc.sbuf_tensor(f"vrab{i}", [128, 32 * 65], BF)) for i in range(2)]
    attl = [es2.enter_context(nc.sbuf_tensor(f"attl{i}", [128, 512], BF)) for i in range(3)]
    attg = [es2.enter_context(nc.sbuf_tensor(f"attg{i}", [128, 1024], BF)) for i in range(3)]
    rowf = es2.enter_context(nc.sbuf_tensor("rowf", [1, 512], F32))
    rowf2 = es2.enter_context(nc.sbuf_tensor("rowf2", [1, 512], F32))
    rowg = es2.enter_context(nc.sbuf_tensor("rowg", [1, 512], F32))
    rowgb = es2.enter_context(nc.sbuf_tensor("rowgb", [1, 512], BF))
    rowb = es2.enter_context(nc.sbuf_tensor("rowb", [1, 512], BF))
    bcl = es2.enter_context(nc.sbuf_tensor("bcl", [128, 512], F32))
    bcg = es2.enter_context(nc.sbuf_tensor("bcg", [128, 512], F32))
    cgn = es2.enter_context(nc.sbuf_tensor("cgn", [64, 512], BF))
    ctmp = es2.enter_context(nc.sbuf_tensor("ctmp", [128, 512], F32))
    es2p = ExitStack()
    P_sc = [es2p.enter_context(nc.psum_tensor(f"Psc{i}", [128, 1024], F32)) for i in range(2)]
    P_ctx = es2p.enter_context(nc.psum_tensor("P_ctx", [128, 512], F32))
    P_rsbc = es2p.enter_context(nc.psum_tensor("P_rsbc", [128, 512], F32))
    P_cg = es2p.enter_context(nc.psum_tensor("P_cg", [128, 512], F32))
    P_G = es2p.enter_context(nc.psum_tensor("P_G", [128, 512], F32))

    def sc_slot(idx):
        i = idx % 4
        return P_sc[i // 2][:, (i % 2) * 512:(i % 2 + 1) * 512]

    hload_free = [0, 0]
    sc_free = [(None, 0)] * 4
    attl_free = [0, 0, 0]
    attg_free = [0, 0, 0]
    ctx_ps_free = 0
    rsbc_ps_free = 0
    cg_ps_free = 0
    G_ps_free = 0

    def band_pairs(tc):
        t0c = tc * 512
        pairs = []
        for s0 in range(max(0, t0c - 512), min(t0c + 512, 2048), 128):
            pairs.append(("own", s0, t0c - s0))
        if tc == 0:
            for j in range(4):
                sv0 = -512 + j * 128
                pairs.append(("tail", sv0, -sv0))
        return pairs

    LAG = 2
    DF = Defer(block)

    pw_int = [es2.enter_context(nc.sbuf_tensor(f"pwi{i}", [128, NCT * 128], I32)) for i in range(2)]
    pw_bf = [es2.enter_context(nc.sbuf_tensor(f"pwb{i}", [128, NCT * 128], BF)) for i in range(2)]
    pw_f32 = [pw_int[i].ap().bitcast(F32) for i in range(2)]
    s_pc = Sem(es, nc, "s_pc")
    pw_cols = ([("g", j, 0) for j in range(NHT)] + [("u", j, 0) for j in range(NHT)] +
               [("d", j, q) for j in range(NCT) for q in range(4)] +
               [("o", j, 0) for j in range(NCT)])
    pw_state = {"i": 0, "ld_free": [0, 0], "st_free": [0, 0]}

    def emit_precast(nmax):
        for _ in range(nmax):
            if pw_state["i"] >= len(pw_cols):
                return
            j = pw_state["i"]
            pw_state["i"] += 1
            kind4, col, q = pw_cols[j]
            buf = j % 2
            ncols = 11 * 128 if kind4 == "d" else NCT * 128

            def _pw_ld(sync, kind4=kind4, col=col, q=q, buf=buf, lf=pw_state["ld_free"][buf]):
                _w(sync, s_dv, lf)
                if kind4 == "g":
                    sync.dma_start(pw_int[buf].ap().rearrange("p (ct f) -> p ct f", ct=NCT),
                                   dwgT[:, col * 128:(col + 1) * 128].rearrange("(ct p) f -> p ct f", p=128)
                                   ).then_inc(s_pc.h, 16)
                elif kind4 == "u":
                    sync.dma_start(pw_int[buf].ap().rearrange("p (ct f) -> p ct f", ct=NCT),
                                   dwuT[:, col * 128:(col + 1) * 128].rearrange("(ct p) f -> p ct f", p=128)
                                   ).then_inc(s_pc.h, 16)
                elif kind4 == "d":
                    sync.dma_start(pw_int[buf].ap().rearrange("p (ct f) -> p ct f", ct=NCT)[:, 0:11, :],
                                   dwdT[q * 11 * 128:(q + 1) * 11 * 128, col * 128:(col + 1) * 128]
                                   .rearrange("(ht p) f -> p ht f", p=128)
                                   ).then_inc(s_pc.h, 16)
                else:
                    sync.dma_start(pw_f32[buf].rearrange("p (ct f) -> p ct f", ct=NCT),
                                   dwoT[:, col * 128:(col + 1) * 128].rearrange("(ct p) f -> p ct f", p=128)
                                   ).then_inc(s_pc.h, 16)
            DF.y(_pw_ld)
            s_pc.n += 16
            pw_ld_done = s_pc.n

            def _pw_cast(v, kind4=kind4, buf=buf, ncols=ncols, pld=pw_ld_done,
                         sf=pw_state["st_free"][buf]):
                _w(v, s_pc, pld)
                _w(v, s_st, sf)
                if kind4 == "o":
                    v.tensor_copy(pw_bf[buf][:, 0:ncols], pw_f32[buf][:, 0:ncols]).then_inc(s_dv.h, 1)
                else:
                    v.tensor_scalar(pw_bf[buf][:, 0:ncols], pw_int[buf][:, 0:ncols],
                                    -128.0, None, ALU.add).then_inc(s_dv.h, 1)
            DF.v(_pw_cast)
            s_dv.n += 1
            pw_state["ld_free"][buf] = s_dv.n
            pw_cast_done = s_dv.n

            def _pw_st(g, kind4=kind4, col=col, q=q, buf=buf, ncols=ncols, pcd=pw_cast_done):
                _w(g, s_dv, pcd)
                if kind4 == "d":
                    g.dma_start(scr_wdbf[col][:, q * 1408:(q + 1) * 1408],
                                pw_bf[buf][:, 0:ncols]).then_inc(s_st.h, 16)
                else:
                    dst = {"g": scr_wgbf, "u": scr_wubf, "o": scr_wobf}[kind4]
                    g.dma_start(dst[col], pw_bf[buf][:, 0:ncols]).then_inc(s_st.h, 16)
            DF.g(_pw_st)
            s_st.n += 16
            pw_state["st_free"][buf] = s_st.n
    tail_q = []
    tail_state = {}
    frees = {"ctx": 0, "rsbc": 0, "cg": 0, "cg_ac": 0, "G": 0}

    for h in range(H):
        hb = h % 2

        def _hload(sync, h=h, hb=hb, hf=hload_free[hb]):
            _w(sync, s_mm, hf)
            sync.dma_start(qb2[hb][:], scr_q[h]).then_inc(s_ld.h, 16)
            sync.dma_start(kb2[hb][:, 0:2048], scr_k[h][:, 0:2048]).then_inc(s_ld.h, 16)
            sync.dma_start(kb2[hb][:, 2048:2560], scr_k[h][:, 3584:4096]).then_inc(s_ld.h, 16)
            sync.dma_start(vsb2[hb][:], scr_vs[h]).then_inc(s_ld.h, 16)
            sync.dma_start(krb2[hb][0:64, :], scr_kr[h]).then_inc(s_ld.h, 16)
            sync.dma_start(krb2[hb][64:128, :], scr_kr[h]).then_inc(s_ld.h, 16)
            sync.dma_start(qrb2[hb][0:64, :], scr_qr[h]).then_inc(s_ld.h, 16)
            sync.dma_start(qrb2[hb][64:128, :], scr_qr[h]).then_inc(s_ld.h, 16)
            sync.dma_start(vrab2[hb][:], scr_vra[h]).then_inc(s_ld.h, 16)
        DF.y(_hload)
        s_ld.n += 9 * 16
        h_ld = s_ld.n

        for tc in range(4):
            t0c = tc * 512
            pairs = band_pairs(tc)
            n_l = len(pairs)
            n_items = n_l + 16

            exp_ready = [None] * n_items
            ab_of = [None] * n_items
            slot_of = [None] * n_items
            geo = [None] * n_items
            slot_box = [0]
            pending = list(tail_q)
            tail_q.clear()

            def pop_pending(pending=pending):
                if pending:
                    pending.pop(0)()

            def emit_scores(i, hb=hb, t0c=t0c, pairs=pairs, n_l=n_l,
                            geo=geo, slot_of=slot_of, slot_box=slot_box, h_ld=h_ld):
                if i < n_l:
                    pk_kind, s0, off = pairs[i]
                    if pk_kind == "own":
                        kbcol = s0
                        vcol = s0
                        tstart = max(s0, t0c)
                    else:
                        kbcol = 2048 + (s0 + 512)
                        vcol = kbcol
                        tstart = t0c
                    tend = min(s0 + 638, t0c + 511)
                    w = tend - tstart + 1
                    geo[i] = (pk_kind, s0, off, vcol, tstart, w)
                    slot = slot_box[0] % 4
                    slot_box[0] += 1
                    slot_of[i] = slot

                    def _df_tensor_1(tns, i=i, hb=hb, kbcol=kbcol, tstart=tstart, w=w, slot=slot,
                          scf=sc_free[slot], hl=h_ld):
                        if i == 0:
                            _w(tns, s_ld, hl)
                        if scf[0] is not None:
                            _w(tns, scf[0], scf[1])
                        tns.matmul(sc_slot(slot)[:, 0:w],
                                   kb2[hb][:, kbcol:kbcol + 128], qb2[hb][:, tstart:tstart + w],
                                   start=True, stop=True).then_inc(s_mm.h, 1)
                    DF.t(_df_tensor_1)
                    s_mm.n += 1
                else:
                    sgp = i - n_l
                    if slot_box[0] % 2:
                        slot_box[0] += 1
                    slot2 = slot_box[0] % 4
                    slot3 = (slot_box[0] + 1) % 4
                    slot_box[0] += 2
                    slot_of[i] = slot2

                    def _df_tensor_2(tns, hb=hb, t0c=t0c, sgp=sgp, slot2=slot2, slot3=slot3,
                          sf2=sc_free[slot2], sf3=sc_free[slot3]):
                        for sf in (sf2, sf3):
                            if sf[0] is not None:
                                _w(tns, sf[0], sf[1])
                        tns.matmul(sc_slot(slot2), krb2[hb][0:64, (2 * sgp) * 128:(2 * sgp + 1) * 128],
                                   qrb2[hb][0:64, t0c:t0c + 512], start=True, stop=True,
                                   tile_position=(0, 0))
                        tns.matmul(sc_slot(slot3), krb2[hb][64:128, (2 * sgp + 1) * 128:(2 * sgp + 2) * 128],
                                   qrb2[hb][64:128, t0c:t0c + 512], start=True,
                                   stop=True, tile_position=(64, 0)).then_inc(s_mm.h, 1)
                    DF.t(_df_tensor_2)
                    s_mm.n += 1
                return s_mm.n

            def emit_exp(i, sc_done, n_l=n_l, geo=geo, slot_of=slot_of,
                         ab_of=ab_of, exp_ready=exp_ready):
                if i < n_l:
                    (pk_kind, s0, off, vcol, tstart, w) = geo[i]
                    ai = i % 3
                    ab_of[i] = ai

                    def _df_scalar_1(s, w=w, ai=ai, af=attl_free[ai], sd=sc_done, slot=slot_of[i]):
                        _w(s, s_mm, max(sd, af))
                        s.activation(attl[ai][:, 0:w], sc_slot(slot)[:, 0:w], AF.Exp,
                                     bias=CST_SHIFT_L).then_inc(s_ac.h, 1)
                    DF.s(_df_scalar_1)
                    s_ac.n += 1
                    exp_done = s_ac.n
                    sc_free[slot_of[i]] = (s_ac, exp_done)

                    nmask = 0
                    if pk_kind == "own" and off <= 0:
                        def _df_vector_1(v, ai=ai, w=w, ed=exp_done):
                            _w(v, s_ac, ed)
                            mw = min(128, w)
                            v.tensor_mul(attl[ai][:, 0:mw], attl[ai][:, 0:mw],
                                         masklo[:, 0:mw]).then_inc(s_dv.h, 1)
                        DF.v(_df_vector_1)
                        s_dv.n += 1
                        nmask = 1
                    elif pk_kind == "own" and off >= 128:
                        colhi = 512 - off
                        def _df_vector_2(v, ai=ai, w=w, colhi=colhi, ed=exp_done):
                            _w(v, s_ac, ed)
                            mw = w - colhi
                            v.tensor_mul(attl[ai][:, colhi:colhi + mw], attl[ai][:, colhi:colhi + mw],
                                         maskhi[:, 0:mw]).then_inc(s_dv.h, 1)
                        DF.v(_df_vector_2)
                        s_dv.n += 1
                        nmask = 1
                    elif pk_kind == "tail":
                        j = (s0 + 512) // 128
                        def _df_vector_3(v, ai=ai, w=w, j=j, ed=exp_done):
                            _w(v, s_ac, ed)
                            v.tensor_mul(attl[ai][:, 0:w], attl[ai][:, 0:w],
                                         bmask[:, j * 512:j * 512 + w]).then_inc(s_dv.h, 1)
                        DF.v(_df_vector_3)
                        s_dv.n += 1
                        nmask = 1
                    exp_ready[i] = (s_dv, s_dv.n) if nmask else (s_ac, exp_done)
                else:
                    gi = i % 3
                    ab_of[i] = gi
                    slot2 = slot_of[i]

                    def _df_scalar_2(s, slot2=slot2, gi=gi, af=attg_free[gi], sd=sc_done):
                        _w(s, s_mm, max(sd, af))
                        s.activation(attg[gi][:], P_sc[slot2 // 2][:], AF.Exp,
                                     bias=CST_SHIFT_L).then_inc(s_ac.h, 1)
                    DF.s(_df_scalar_2)
                    s_ac.n += 1
                    exp_done = s_ac.n
                    sc_free[slot2] = (s_ac, exp_done)
                    sc_free[(slot2 + 1) % 4] = (s_ac, exp_done)
                    exp_ready[i] = (s_ac, exp_done)

            def emit_ctx(i, hb=hb, t0c=t0c, n_l=n_l, geo=geo, ab_of=ab_of,
                         exp_ready=exp_ready):
                sem, cnt = exp_ready[i]
                ai = ab_of[i]
                if i < n_l:
                    (pk_kind, s0, off, vcol, tstart, w) = geo[i]

                    def _df_tensor_3(tns, i=i, hb=hb, vcol=vcol, tstart=tstart, t0c=t0c, w=w, ai=ai,
                          sem=sem, cnt=cnt, n_l=n_l,
                          cpf=frees["ctx"], rbf=frees["rsbc"]):
                        if i == 0:
                            _w(tns, s_dv, cpf)
                            _w(tns, s_ac, rbf)
                        _w(tns, sem, cnt)
                        trel = tstart - t0c
                        tns.matmul(P_ctx[:, trel:trel + w], vsb2[hb][:, vcol:vcol + 128],
                                   attl[ai][:, 0:w], start=(i == 0), stop=(i == n_l - 1))
                        tns.matmul(P_rsbc[0:1, trel:trel + w], onescol[:, 0:1],
                                   attl[ai][:, 0:w], start=(i == 0),
                                   stop=(i == n_l - 1)).then_inc(s_mm.h, 1)
                    DF.t(_df_tensor_3)
                    s_mm.n += 1
                    attl_free[ai] = s_mm.n
                else:
                    sgp = i - n_l

                    def _df_tensor_4(tns, hb=hb, sgp=sgp, ai=ai, sem=sem, cnt=cnt,
                          cgf=frees["cg"], cga=frees["cg_ac"]):
                        if sgp == 0:
                            _w(tns, s_dv, cgf)
                            _w(tns, s_ac, cga)
                        _w(tns, sem, cnt)
                        tns.matmul(P_cg[0:65, 0:512], vrab2[hb][:, (2 * sgp) * 65:(2 * sgp + 1) * 65],
                                   attg[ai][:, 0:512], start=(sgp == 0), stop=False)
                        tns.matmul(P_cg[0:65, 0:512], vrab2[hb][:, (2 * sgp + 1) * 65:(2 * sgp + 2) * 65],
                                   attg[ai][:, 512:1024], start=False,
                                   stop=(sgp == 15)).then_inc(s_mm.h, 1)
                    DF.t(_df_tensor_4)
                    s_mm.n += 1
                    attg_free[ai] = s_mm.n

            for i in range(n_items):
                sc_done = emit_scores(i)
                emit_exp(i, sc_done)
                if i in (0, 5, 10):
                    pop_pending()
                if i >= LAG:
                    emit_ctx(i - LAG)
            for j in range(max(0, n_items - LAG), n_items):
                emit_ctx(j)
            while pending:
                pop_pending()

            lmd_rs = s_mm.n

            def _df_scalar_recips(s, lmd=lmd_rs):
                _w(s, s_mm, lmd)
                s.activation(rowf[0:1, 0:512], P_rsbc[0:1, 0:512], AF.Ln)
                s.activation(rowf2[0:1, 0:512], P_cg[64:65, 0:512], AF.Ln)
                s.activation(rowb[0:1, 0:512], rowf[0:1, 0:512],
                             AF.Exp, scale=-1.0).then_inc(s_ac.h, 1)
                s.activation(rowg[0:1, 0:512], rowf2[0:1, 0:512],
                             AF.Exp, scale=-1.0).then_inc(s_ac.h, 1)
            DF.s(_df_scalar_recips)
            s_ac.n += 2
            recipl_cnt = s_ac.n - 1
            recipg_cnt = s_ac.n
            frees["rsbc"] = recipl_cnt
            frees["cg_ac"] = recipg_cnt

            emit_precast(3)
            invl_done = s_dv.n
            cg_mm_done = s_mm.n
            hload_free[hb] = cg_mm_done

            def _df_vector_invg(v, tc=tc, rg=recipg_cnt):
                _w(v, s_ac, rg)
                v.tensor_mul(rowgb[:], rowg[:],
                             gate_row[0:1, tc * 512:(tc + 1) * 512]).then_inc(s_dv.h, 1)
            DF.v(_df_vector_invg)
            s_dv.n += 1
            invg_done = s_dv.n
            if tc % 2 == 1:
                DF.flush()

            def t1(invl_done=invl_done, rb=recipl_cnt, hb=hb, h=h, tc=tc):
                slot = slot_box[0] % 4
                slot_box[0] += 1
                sf = sc_free[slot]

                def _t1_mm(tns, slot=slot, sf=sf, invl_done=invl_done, rb=rb):
                    if sf[0] is not None:
                        _w(tns, sf[0], sf[1])
                    _w(tns, s_dv, invl_done)
                    _w(tns, s_ac, rb)
                    tns.matmul(sc_slot(slot), onesrow[0:1, :], rowb[0:1, :],
                               start=True, stop=True).then_inc(s_mm.h, 1)
                DF.t(_t1_mm)
                s_mm.n += 1
                bcl_mm = s_mm.n

                base1 = s_dv.n

                def _t1_v(v, slot=slot, bcl_mm=bcl_mm, base=base1):
                    _w(v, s_mm, bcl_mm)
                    v.tensor_copy(bcl[:], sc_slot(slot)).then_inc(s_dv.h, 1)
                    v.wait_ge(s_dv.h, base + 1)
                    v.tensor_mul(ctmp[:], P_ctx[:, 0:512], bcl[:]).then_inc(s_dv.h, 1)
                DF.v(_t1_v)
                s_dv.n += 2
                sc_free[slot] = (s_dv, s_dv.n - 1)
                frees["ctx"] = s_dv.n
                tail_state["ctmp"] = s_dv.n

            def t2(hb=hb, h=h, tc=tc, ig=invg_done):
                slot = slot_box[0] % 4
                slot_box[0] += 1
                sf = sc_free[slot]

                def _t2_mm(tns, slot=slot, sf=sf, ig=ig):
                    if sf[0] is not None:
                        _w(tns, sf[0], sf[1])
                    _w(tns, s_dv, ig)
                    tns.matmul(sc_slot(slot), onesrow[0:1, :], rowgb[0:1, :],
                               start=True, stop=True).then_inc(s_mm.h, 1)
                DF.t(_t2_mm)
                s_mm.n += 1
                bcg_mm = s_mm.n

                base2 = s_dv.n

                def _t2_v(v, slot=slot, bcg_mm=bcg_mm, base=base2):
                    _w(v, s_mm, bcg_mm)
                    v.tensor_copy(bcg[:], sc_slot(slot)).then_inc(s_dv.h, 1)
                    v.wait_ge(s_dv.h, base + 1)
                    v.tensor_mul(cgn[0:64, :], P_cg[0:64, 0:512], bcg[0:64, :]).then_inc(s_dv.h, 1)
                DF.v(_t2_v)
                s_dv.n += 2
                sc_free[slot] = (s_dv, s_dv.n - 1)
                frees["cg"] = s_dv.n
                tail_state["cgn"] = s_dv.n

            def t3(h=h, tc=tc):
                def _t3_mm(tns, cd=tail_state["cgn"], gf=frees["G"]):
                    _w(tns, s_dv, max(cd, gf))
                    tns.matmul(P_G[:, 0:512], uoTb[:], cgn[0:64, :],
                               start=True, stop=True).then_inc(s_mm.h, 1)
                DF.t(_t3_mm)
                s_mm.n += 1
                G_mm = s_mm.n

                def _t3_v(v, h=h, tc=tc, G_mm=G_mm, cm=tail_state["ctmp"]):
                    _w(v, s_mm, G_mm)
                    v.tensor_add(ctx_all[:, h * TLOC + tc * 512:h * TLOC + (tc + 1) * 512],
                                 ctmp[:], P_G[:, 0:512]).then_inc(s_dv.h, 1)
                DF.v(_t3_v)
                s_dv.n += 1
                frees["G"] = s_dv.n

            tail_q.append(t1)
            tail_q.append(t2)
            tail_q.append(t3)

    for fn in tail_q:
        fn()
    tail_q.clear()
    emit_precast(len(pw_cols))
    DF.flush()

    if debug:
        @block.gpsimd
        def _(g):
            _w(g, s_dv, s_dv.n)
            g.dma_start(dbg["ctx"][:], ctx_all[:]).then_inc(s_st.h, 16)
        s_st.n += 16

    barrier()
    es2.close()
    es2p.close()

    if phases <= 2:
        return _finish(nc, blk, block, es, (s_ld, s_st, s_mm, s_dv, s_ac)), dbg

    es3 = ExitStack()
    wobf = [es3.enter_context(nc.sbuf_tensor(f"wobf{i}", [128, NCT * 128], BF)) for i in range(2)]
    aoT = es3.enter_context(nc.sbuf_tensor("aoT", [128, NCT * 512], F32))
    xtile = [es3.enter_context(nc.sbuf_tensor(f"p3x{i}", [128, C], F32)) for i in range(2)]
    x1b = es3.enter_context(nc.sbuf_tensor("x1b", [128, C], F32))
    xn2b = es3.enter_context(nc.sbuf_tensor("xn2b", [128, C], BF))
    st3 = es3.enter_context(nc.sbuf_tensor("st3", [128, 24], F32))
    tmp3 = es3.enter_context(nc.sbuf_tensor("tmp3", [128, 16], F32))
    xn2st = es3.enter_context(nc.sbuf_tensor("xn2st", [128, NCT * 512], BF))
    es3p = ExitStack()
    p3_po = [es3p.enter_context(nc.psum_tensor(f"p3po{i}", [128, 512], F32)) for i in range(2)]
    p3_tr = es3p.enter_context(nc.psum_tensor("p3tr", [128, 2048], F32))
    p3_trb = es3p.enter_context(nc.psum_tensor("p3trb", [128, 2048], BF))

    po_free = [0, 0]
    wobf_free = [0, 0]
    tr_free = 0
    trb_free = 0
    xtile_free = [0, 0]
    x1b_free = 0
    xn2b_free = 0

    for tc in range(4):
        for co in range(NCT):
            pb = co % 2
            wb3 = co % 2

            @block.sync
            def _(sync, co=co, wb3=wb3, wf=wobf_free[wb3]):
                _w(sync, s_mm, wf)
                sync.dma_start(wobf[wb3][:], scr_wobf[co]).then_inc(s_ld.h, 16)
            s_ld.n += 16
            wld3 = s_ld.n

            @block.tensor
            def _(tns, tc=tc, co=co, pb=pb, wb3=wb3, pf=po_free[pb], wc=wld3):
                _w(tns, s_ld, wc)
                _w(tns, s_dv, pf)
                for kt in range(NCT):
                    i = tns.matmul(p3_po[pb][:], wobf[wb3][:, kt * 128:(kt + 1) * 128],
                                   ctx_all[:, kt * TLOC + tc * 512:kt * TLOC + tc * 512 + 512],
                                   start=(kt == 0), stop=(kt == NCT - 1))
                i.then_inc(s_mm.h, 1)
            s_mm.n += 1
            po_done = s_mm.n
            wobf_free[wb3] = po_done

            @block.vector
            def _(v, co=co, pb=pb):
                _w(v, s_mm, po_done)
                v.tensor_copy(aoT[:, co * 512:(co + 1) * 512], p3_po[pb][:]).then_inc(s_dv.h, 1)
            s_dv.n += 1
            po_free[pb] = s_dv.n
        ao_done = s_dv.n

        for tti in range(4):
            xb = tti % 2
            trow = tc * 512 + tti * 128

            @block.sync
            def _(sync, xb=xb, trow=trow, xf=xtile_free[xb]):
                _w(sync, s_dv, xf)
                sync.dma_start(xtile[xb][:], dx[trow:trow + 128, :]).then_inc(s_ld.h, 16)
            s_ld.n += 16
            x_ld = s_ld.n

            @block.tensor
            def _(tns, tti=tti, tf=tr_free):
                _w(tns, s_dv, max(ao_done, tf))
                for co in range(NCT):
                    i = tns.transpose(p3_tr[:, co * 128:(co + 1) * 128],
                                      aoT[:, co * 512 + tti * 128:co * 512 + tti * 128 + 128],
                                      identF[:])
                i.then_inc(s_mm.h, 1)
            s_mm.n += 1
            tr_done = s_mm.n

            add_cnt = s_dv.n + 1

            @block.vector
            def _(v, xb=xb, x1f=x1b_free, ac=add_cnt):
                _w(v, s_mm, tr_done)
                _w(v, s_ld, x_ld)
                _w(v, s_st, x1f)
                v.tensor_add(x1b[:], p3_tr[:], xtile[xb][:]).then_inc(s_dv.h, 1)
                v.wait_ge(s_dv.h, ac)
                _ln_stats(v, x1b, st3, tmp3, vchain).then_inc(s_dv.h, 1)
            s_dv.n += 2
            tr_free = s_dv.n
            xtile_free[xb] = s_dv.n
            var3_done = s_dv.n

            @block.scalar
            def _(s):
                _w(s, s_dv, var3_done)
                s.activation(tmp3[:, 7:8], tmp3[:, 5:6], AF.Sqrt, bias=CST_EPS).then_inc(s_ac.h, 1)
            s_ac.n += 1
            sd3_done = s_ac.n

            @block.vector
            def _(v, xnf=xn2b_free):
                _w(v, s_ac, sd3_done)
                _w(v, s_mm, xnf)
                vchain(v.reciprocal(tmp3[:, 6:7], tmp3[:, 7:8]))
                v.tensor_scalar(xn2b[:], x1b[:], tmp3[:, 2:3], tmp3[:, 6:7],
                                ALU.subtract, ALU.mult).then_inc(s_dv.h, 1)
            s_dv.n += 1
            xn2_done = s_dv.n

            @block.gpsimd
            def _(g, trow=trow):
                _w(g, s_dv, xn2_done)
                g.dma_start(scr_x1[trow:trow + 128, :], x1b[:]).then_inc(s_st.h, 16)
                if debug:
                    g.dma_start(dbg["x1"][trow:trow + 128, :], x1b[:]).then_inc(s_st.h, 16)
            s_st.n += 32 if debug else 16
            x1_st = s_st.n
            x1b_free = x1_st

            @block.tensor
            def _(tns, tbf=trb_free):
                _w(tns, s_dv, max(xn2_done, tbf))
                for ct in range(NCT):
                    i = tns.transpose(p3_trb[:, ct * 128:(ct + 1) * 128],
                                      xn2b[:, ct * 128:(ct + 1) * 128], identB[:])
                i.then_inc(s_mm.h, 1)
            s_mm.n += 1
            trb_done = s_mm.n
            xn2b_free = trb_done

            @block.vector
            def _(v, tti=tti):
                _w(v, s_mm, trb_done)
                for ct in range(NCT):
                    i = v.tensor_copy(xn2st[:, ct * 512 + tti * 128:ct * 512 + tti * 128 + 128],
                                      p3_trb[:, ct * 128:(ct + 1) * 128])
                i.then_inc(s_dv.h, 1)
            s_dv.n += 1
            trb_free = s_dv.n
        chunk_evac_done = s_dv.n

        @block.gpsimd
        def _(g, tc=tc):
            _w(g, s_dv, chunk_evac_done)
            g.dma_start(scr_xn2[tc].rearrange("ct p t -> p ct t"),
                        xn2st.ap().rearrange("p (ct t) -> p ct t", ct=NCT)).then_inc(s_st.h, 16)
            if debug:
                g.dma_start(dbg["xn2"][tc].rearrange("ct p t -> p ct t"),
                            xn2st.ap().rearrange("p (ct t) -> p ct t", ct=NCT)).then_inc(s_st.h, 16)
        s_st.n += 32 if debug else 16
        xn2st_free = s_st.n

        @block.vector
        def _(v, f=xn2st_free):
            _w(v, s_st, f)
            v.engine_nop().then_inc(s_dv.h, 1)
        s_dv.n += 1
        trb_free = s_dv.n

    barrier()
    es3.close()
    es3p.close()
    es_c.close()

    if phases <= 3:
        return _finish(nc, blk, block, es, (s_ld, s_st, s_mm, s_dv, s_ac)), dbg

    es4 = ExitStack()
    sgb = es4.enter_context(nc.sbuf_tensor("sgb", [128, NHT], F32))
    sub = es4.enter_context(nc.sbuf_tensor("sub", [128, NHT], F32))
    sdb = es4.enter_context(nc.sbuf_tensor("sdb", [128, NCT], F32))
    xnf = es4.enter_context(nc.sbuf_tensor("xnf", [128, NCT * 1024], BF))
    wgbf = [es4.enter_context(nc.sbuf_tensor(f"wgbf{i}", [128, NCT * 128], BF)) for i in range(2)]
    wubf = [es4.enter_context(nc.sbuf_tensor(f"wubf{i}", [128, NCT * 128], BF)) for i in range(2)]
    silu_sb = [es4.enter_context(nc.sbuf_tensor(f"silu{i}", [128, 512], F32)) for i in range(2)]
    hstage = [es4.enter_context(nc.sbuf_tensor(f"hstage{i}", [128, 1024], BF)) for i in range(2)]
    hchunk = es4.enter_context(nc.sbuf_tensor("hchunk", [128, NHT * 512], BF))
    wdbf = [es4.enter_context(nc.sbuf_tensor(f"wdbf{i}", [128, NHT * 128], BF)) for i in range(2)]
    dst_sb = [es4.enter_context(nc.sbuf_tensor(f"dst{i}", [128, 512], BF)) for i in range(2)]
    asm_sb = [es4.enter_context(nc.sbuf_tensor(f"asm{i}", [128, NCT * 128], BF)) for i in range(2)]
    x1t = [es4.enter_context(nc.sbuf_tensor(f"x1t{i}", [128, C], F32)) for i in range(2)]
    x2b = es4.enter_context(nc.sbuf_tensor("x2b", [128, C], F32))
    es4p = ExitStack()
    p4_g = [es4p.enter_context(nc.psum_tensor(f"p4g{i}", [128, 512], F32)) for i in range(2)]
    p4_u = [es4p.enter_context(nc.psum_tensor(f"p4u{i}", [128, 512], F32)) for i in range(2)]
    p4_d = [es4p.enter_context(nc.psum_tensor(f"p4d{i}", [128, 512], F32)) for i in range(2)]
    p4_trb = es4p.enter_context(nc.psum_tensor("p4trb", [128, 2048], BF))

    DF4 = Defer(block)

    @block.sync
    def _(sync):
        sync.dma_start(sgb[:], dsg[:]).then_inc(s_ld.h, 16)
        sync.dma_start(sub[:], dsu[:]).then_inc(s_ld.h, 16)
        sync.dma_start(sdb[:], dsd[:]).then_inc(s_ld.h, 16)
    s_ld.n += 48
    sc_ld = s_ld.n

    wgbf_free = [0, 0]
    g_free = [0, 0]
    u_free = [0, 0]
    silu_free = [0, 0]
    hstage_free = [0, 0]
    xnf_free = 0
    hchunk_ld_prev = 0
    wdbf_free = [0, 0]
    d_free = [0, 0]
    dst_free = [0, 0]
    asm_free = [0, 0]
    scr_d_rd_prev = 0
    x1t_free = [0, 0]
    x2b_free = 0

    for tb in range(2):
        def _df4_y_1(sync, tb=tb, xf=xnf_free):
            if tb == 0:
                _w(sync, s_ld, sc_ld)
            _w(sync, s_mm, xf)
            for sch in range(2):
                sync.dma_start(
                    xnf.ap().rearrange("p (ct s t) -> p ct s t", ct=NCT, s=2)[:, :, sch, :],
                    scr_xn2[2 * tb + sch].rearrange("ct p t -> p ct t")).then_inc(s_ld.h, 16)
        DF4.y(_df4_y_1)
        s_ld.n += 32
        xnf_ld = s_ld.n

        for ht in range(NHT):
            wb = ht % 2

            def _df4_y_2(sync, ht=ht, wb=wb, bf=wgbf_free[wb]):
                _w(sync, s_mm, bf)
                sync.dma_start(wgbf[wb][:], scr_wgbf[ht]).then_inc(s_ld.h, 16)
                sync.dma_start(wubf[wb][:], scr_wubf[ht]).then_inc(s_ld.h, 16)
            DF4.y(_df4_y_2)
            s_ld.n += 32
            wcast = s_ld.n

            for tc2 in range(2):
                pb = tc2

                def _df4_t_6(tns, wb=wb, tc2=tc2, pb=pb, gf=g_free[pb], uf=u_free[pb], xl=xnf_ld, wcast=wcast):
                    if tc2 == 0:
                        _w(tns, s_ld, max(wcast, xl))
                    _w(tns, s_ac, gf)
                    for kt in range(NCT):
                        tns.matmul(p4_g[pb][:], wgbf[wb][:, kt * 128:(kt + 1) * 128],
                                   xnf[:, kt * 1024 + tc2 * 512:kt * 1024 + tc2 * 512 + 512],
                                   start=(kt == 0), stop=(kt == NCT - 1))
                    _w(tns, s_dv, uf)
                    for kt in range(NCT):
                        i = tns.matmul(p4_u[pb][:], wubf[wb][:, kt * 128:(kt + 1) * 128],
                                       xnf[:, kt * 1024 + tc2 * 512:kt * 1024 + tc2 * 512 + 512],
                                       start=(kt == 0), stop=(kt == NCT - 1))
                    i.then_inc(s_mm.h, 1)
                DF4.t(_df4_t_6)
                s_mm.n += 1
                gu_done = s_mm.n
                if tc2 == 1:
                    wgbf_free[wb] = gu_done

                def _df4_s_9(s, ht=ht, pb=pb, sf=silu_free[pb], gu_done=gu_done):
                    _w(s, s_mm, gu_done)
                    _w(s, s_dv, sf)
                    s.activation(silu_sb[pb][:], p4_g[pb][:], AF.Silu,
                                 scale=sgb[:, ht:ht + 1]).then_inc(s_ac.h, 1)
                DF4.s(_df4_s_9)
                s_ac.n += 1
                g_free[pb] = s_ac.n
                silu_done = s_ac.n

                def _df4_v_11(v, ht=ht, tc2=tc2, pb=pb, wb=wb, hf=hstage_free[wb], silu_done=silu_done):
                    _w(v, s_ac, silu_done)
                    if tc2 == 0:
                        _w(v, s_st, hf)
                    v.scalar_tensor_tensor(hstage[wb][:, tc2 * 512:(tc2 + 1) * 512],
                                           p4_u[pb][:], sub[:, ht:ht + 1], silu_sb[pb][:],
                                           ALU.mult, ALU.mult).then_inc(s_dv.h, 1)
                DF4.v(_df4_v_11)
                s_dv.n += 1
                u_free[pb] = s_dv.n
                silu_free[pb] = s_dv.n
            hst_done = s_dv.n

            def _df4_g_15(g, ht=ht, wb=wb, hp=hchunk_ld_prev, tb=tb, hst_done=hst_done):
                _w(g, s_dv, hst_done)
                if tb == 1 and ht < 2:
                    _w(g, s_ld, hp)
                g.dma_start(scr_h[ht], hstage[wb][:]).then_inc(s_st.h, 16)
                if debug:
                    g.dma_start(dbg["h"][tb][ht], hstage[wb][:]).then_inc(s_st.h, 16)
            DF4.g(_df4_g_15)
            s_st.n += 32 if debug else 16
            hstage_free[wb] = s_st.n
        hstore_done = s_st.n
        xnf_free = s_mm.n

        for tc2 in range(2):
            def _df4_y_3(sync, tc2=tc2, hs=hstore_done, sdp=scr_d_rd_prev, tb=tb,
                  mmg=s_mm.n):
                _w(sync, s_st, hs)
                _w(sync, s_mm, mmg)
                if tc2 == 0 and tb == 1:
                    _w(sync, s_ld, sdp)
                sync.dma_start(hchunk.ap().rearrange("p (ht t) -> p ht t", ht=NHT),
                               scr_h.rearrange("ht p t -> p ht t")[:, :, tc2 * 512:(tc2 + 1) * 512]
                               ).then_inc(s_ld.h, 16)
            DF4.y(_df4_y_3)
            s_ld.n += 16
            hchunk_ld = s_ld.n
            if tc2 == 1:
                hchunk_ld_prev = hchunk_ld

            for co in range(NCT):
                wdb = co % 2

                def _df4_y_4(sync, co=co, wdb=wdb, wf=wdbf_free[wdb]):
                    _w(sync, s_mm, wf)
                    sync.dma_start(wdbf[wdb][:], scr_wdbf[co]).then_inc(s_ld.h, 16)
                DF4.y(_df4_y_4)
                s_ld.n += 16
                wd_ld = s_ld.n

                pb = co % 2

                def _df4_t_7(tns, co=co, pb=pb, wdb=wdb, df=d_free[pb], hl=hchunk_ld, wd_ld=wd_ld):
                    _w(tns, s_ld, max(wd_ld, hl if co == 0 else 0))
                    _w(tns, s_dv, df)
                    for kt in range(NHT):
                        i = tns.matmul(p4_d[pb][:], wdbf[wdb][:, kt * 128:(kt + 1) * 128],
                                       hchunk[:, kt * 512:(kt + 1) * 512],
                                       start=(kt == 0), stop=(kt == NHT - 1))
                    i.then_inc(s_mm.h, 1)
                DF4.t(_df4_t_7)
                s_mm.n += 1
                d_done = s_mm.n
                wdbf_free[wdb] = d_done

                db = co % 2

                def _df4_v_13(v, co=co, pb=pb, db=db, dsf=dst_free[db], d_done=d_done):
                    _w(v, s_mm, d_done)
                    _w(v, s_st, dsf)
                    v.tensor_scalar(dst_sb[db][:], p4_d[pb][:], sdb[:, co:co + 1],
                                    None, ALU.mult).then_inc(s_dv.h, 1)
                DF4.v(_df4_v_13)
                s_dv.n += 1
                d_free[pb] = s_dv.n
                devac_done = s_dv.n

                def _df4_g_16(g, co=co, tc2=tc2, db=db, tb=tb, devac_done=devac_done):
                    _w(g, s_dv, devac_done)
                    g.dma_start(scr_d[co][:, tc2 * 512:(tc2 + 1) * 512], dst_sb[db][:]).then_inc(s_st.h, 16)
                    if debug:
                        g.dma_start(dbg["dn"][tb][co][:, tc2 * 512:(tc2 + 1) * 512],
                                    dst_sb[db][:]).then_inc(s_st.h, 16)
                DF4.g(_df4_g_16)
                s_st.n += 32 if debug else 16
                dst_free[db] = s_st.n
        dstore_done = s_st.n

        trb4_free = 0
        for tti in range(8):
            xb = tti % 2
            trow = tb * 1024 + tti * 128

            def _df4_y_5(sync, xb=xb, tti=tti, trow=trow, xf=x1t_free[xb], af=asm_free[xb], ds=dstore_done):
                _w(sync, s_dv, xf)
                sync.dma_start(x1t[xb][:], scr_x1[trow:trow + 128, :]).then_inc(s_ld.h, 16)
                _w(sync, s_mm, af)
                _w(sync, s_st, ds)
                sync.dma_start(asm_sb[xb].ap().rearrange("p (co t) -> p co t", co=NCT),
                               scr_d.rearrange("co p t -> p co t")[:, :, (trow % 1024):(trow % 1024) + 128]
                               ).then_inc(s_ld.h, 16)
            DF4.y(_df4_y_5)
            s_ld.n += 32
            x1_ld = s_ld.n
            if tti == 7:
                scr_d_rd_prev = x1_ld

            def _df4_t_8(tns, xb=xb, tf=trb4_free, xl=x1_ld):
                _w(tns, s_ld, xl)
                _w(tns, s_dv, tf)
                for co in range(NCT):
                    i = tns.transpose(p4_trb[:, co * 128:(co + 1) * 128],
                                      asm_sb[xb][:, co * 128:(co + 1) * 128],
                                      identB[:])
                i.then_inc(s_mm.h, 1)
            DF4.t(_df4_t_8)
            s_mm.n += 1
            tr4_done = s_mm.n
            asm_free[xb] = tr4_done

            def _df4_v_14(v, xb=xb, xf=x2b_free, tr4_done=tr4_done):
                _w(v, s_mm, tr4_done)
                _w(v, s_st, xf)
                v.tensor_add(x2b[:], p4_trb[:], x1t[xb][:]).then_inc(s_dv.h, 1)
            DF4.v(_df4_v_14)
            s_dv.n += 1
            trb4_free = s_dv.n
            x1t_free[xb] = s_dv.n
            x2_done = s_dv.n

            def _df4_g_17(g, trow=trow, x2_done=x2_done):
                _w(g, s_dv, x2_done)
                g.dma_start(dout[trow:trow + 128, :], x2b[:]).then_inc(s_st.h, 16)
            DF4.g(_df4_g_17)
            s_st.n += 16
            x2b_free = s_st.n

        DF4.flush()

    return _finish(nc, blk, block, es, (s_ld, s_st, s_mm, s_dv, s_ac)), dbg


def _finish(nc, blk, block, es, sems):
    s_ld, s_st, s_mm, s_dv, s_ac = sems

    @block.sync
    def _(sync):
        for s in sems:
            if s.n > 0:
                sync.wait_ge(s.h, s.n)
    blk.__exit__(None, None, None)
    es.close()
    return nc


def _host_prep(inputs):
    x = np.asarray(inputs["L0_attn_x"], dtype=np.float32)
    w_qkv = np.asarray(inputs["L0_attn_w_qkv"], dtype=np.float32)
    w_o = np.asarray(inputs["L0_attn_w_o"], dtype=np.float32)
    pk = np.asarray(inputs["L0_attn_pk"], dtype=np.float32)
    pv = np.asarray(inputs["L0_attn_pv"], dtype=np.float32)
    uo = np.asarray(inputs["L0_attn_uo"], dtype=np.float32)
    w1 = np.asarray(inputs["L0_attn_info_w1"], dtype=np.float32)
    w2 = np.asarray(inputs["L0_attn_info_w2"], dtype=np.float32)
    wg = np.asarray(inputs["L0_ffn_wq_gate"], dtype=np.int32)
    sg = np.asarray(inputs["L0_ffn_s_gate"], dtype=np.float32)
    wu = np.asarray(inputs["L0_ffn_wq_up"], dtype=np.int32)
    su = np.asarray(inputs["L0_ffn_s_up"], dtype=np.float32)
    wd = np.asarray(inputs["L0_ffn_wq_down"], dtype=np.int32)
    sd = np.asarray(inputs["L0_ffn_s_down"], dtype=np.float32)

    shared = dict(
        wqkvT=np.ascontiguousarray(w_qkv.T),
        woT=np.ascontiguousarray(w_o.T),
        pkT=np.ascontiguousarray(pk.T),
        pvT=np.ascontiguousarray(pv.T),
        uoT=np.ascontiguousarray(uo.T),
        w1T=np.ascontiguousarray(w1.T),
        w2T=np.ascontiguousarray(w2.reshape(4, 128).T),
        wgT=np.ascontiguousarray(wg.T),
        wuT=np.ascontiguousarray(wu.T),
        wdT=np.ascontiguousarray(wd.T),
        sg2=np.ascontiguousarray(sg.reshape(NHT, 128).T),
        su2=np.ascontiguousarray(su.reshape(NHT, 128).T),
        sd2=np.ascontiguousarray(sd.reshape(NCT, 128).T),
        identB=np.eye(128, dtype=np.float32).astype(BF16NP),
        identF=np.eye(128, dtype=np.float32),
    )

    inv = 1.0 / (10000.0 ** (np.arange(0, D, 2, dtype=np.float64) / D))
    tt = np.arange(4096, dtype=np.float64)
    freqs = np.outer(tt, inv)
    emb = np.concatenate([freqs, freqs], axis=1)
    cosT = np.cos(emb).T.astype(np.float32)
    sinT = np.sin(emb).T.astype(np.float32)
    sign = np.where(np.arange(128) % 2 == 0, -1.0, 1.0)[:, None].astype(np.float32)
    sinT = sinT * sign

    sl = np.arange(128)[:, None]
    tl = np.arange(128)[None, :]
    masklo = (tl >= sl).astype(np.float32).astype(BF16NP)
    maskhi = (tl < sl).astype(np.float32).astype(BF16NP)

    cstcol = np.zeros((128, 8), dtype=np.float32)
    cstcol[:, 0] = SHIFT_L
    cstcol[:, 1] = SHIFT_G
    cstcol[:, 2] = EPS
    cstcol[:, 3] = 1.0
    onesrow = np.ones((1, 128), dtype=np.float32).astype(BF16NP)

    in_maps = []
    for core in range(8):
        b = core // 2
        half = core % 2
        T0 = half * TLOC
        P0 = (1 - half) * TLOC
        x_own = np.ascontiguousarray(x[b, T0:T0 + TLOC, :])
        x_par = np.ascontiguousarray(x[b, P0:P0 + TLOC, :])
        order = np.concatenate([np.arange(T0, T0 + TLOC), np.arange(P0, P0 + TLOC)])
        ropeC = np.ascontiguousarray(cosT[:, order])
        ropeS = np.ascontiguousarray(sinT[:, order])
        bm = np.zeros((4, 128, 512), dtype=np.float32)
        if half == 1:
            for j in range(4):
                off = 512 - j * 128
                tcols = np.arange(512)[None, :]
                srows = np.arange(128)[:, None]
                rel = tcols + off - srows
                bm[j] = ((rel >= 0) & (rel <= 511)).astype(np.float32)
        bmask = np.ascontiguousarray(bm.astype(BF16NP))

        m = dict(x_own=x_own, x_par=x_par,
                 ropeC=ropeC, ropeS=ropeS, bmask=bmask,
                 masklo=masklo, maskhi=maskhi, cst=cstcol, onesrow=onesrow,
                 **shared)
        in_maps.append(m)
    return in_maps


def kernel(**inputs):
    in_maps = _host_prep(inputs)
    nc, _ = build(debug=False, phases=5)
    res = run_bass_kernel_spmd(nc, in_maps, core_ids=list(range(8)))
    out = np.empty((B, T, C), dtype=np.float32)
    for core in range(8):
        b = core // 2
        T0 = (core % 2) * TLOC
        out[b, T0:T0 + TLOC, :] = res.results[core]["out"]
    return out


# version 15 (final)
# speedup vs baseline: 1.0464x; 1.0464x over previous
import numpy as np
from contextlib import ExitStack

import concourse.bass as bass
import concourse.mybir as mybir
from concourse.bass_utils import run_bass_kernel_spmd
import ml_dtypes

BF16NP = ml_dtypes.bfloat16
F32 = mybir.dt.float32
BF = mybir.dt.bfloat16
I32 = mybir.dt.int32
AF = mybir.ActivationFunctionType
ALU = mybir.AluOpType

B, T, C = 4, 4096, 2048
H, D, R = 16, 128, 64
HID = 5632
TLOC = 2048
CH = 512
NCT = C // 128
NHT = HID // 128
SCALE_L = 1.0 / float(np.sqrt(D))
SCALE_G = SCALE_L * D / R
SHIFT_L = -24.0
SHIFT_G = -4.0
EPS = 1e-5
PAIRSWAP = [i ^ 1 for i in range(32)]


class Sem:
    def __init__(self, es, nc, name):
        self.h = es.enter_context(nc.semaphore(name))
        self.n = 0


class Defer:

    def __init__(self, block):
        self.block = block
        self.q = {"y": [], "t": [], "s": [], "v": [], "g": []}

    def y(self, f):
        self.q["y"].append(f)

    def t(self, f):
        self.q["t"].append(f)

    def s(self, f):
        self.q["s"].append(f)

    def v(self, f):
        self.q["v"].append(f)

    def g(self, f):
        self.q["g"].append(f)

    def flush(self):
        for f in self.q["y"]:
            self.block.sync(f)
        for f in self.q["t"]:
            self.block.tensor(f)
        for f in self.q["s"]:
            self.block.scalar(f)
        for f in self.q["v"]:
            self.block.vector(f)
        for f in self.q["g"]:
            self.block.gpsimd(f)
        for k in self.q:
            self.q[k] = []


def _w(eng, sem, cnt):
    assert cnt <= sem.n, f"wait on future sem count {cnt} > {sem.n}"
    if cnt > 0:
        eng.wait_ge(sem.h, cnt)


def _ln_stats(v, xsrc, st, tmp, vc):
    for g in range(3):
        v.bn_stats(st[:, g * 6:(g + 1) * 6], xsrc[:, g * 512:(g + 1) * 512])
    vc(v.bn_stats(st[:, 18:24], xsrc[:, 1536:2048]))
    stv = st.ap().rearrange("p (g three) -> p g three", three=3)
    means = stv[:, :, 1]
    cvs = stv[:, :, 2]
    t = tmp
    v.reduce_sum(t[:, 0:1], means, axis=mybir.AxisListType.X)
    v.tensor_tensor(t[:, 8:16], means, means, ALU.mult)
    vc(v.reduce_sum(t[:, 4:5], cvs, axis=mybir.AxisListType.X))
    v.tensor_scalar_mul(t[:, 2:3], t[:, 0:1], 0.125)
    v.reduce_sum(t[:, 3:4], t[:, 8:16], axis=mybir.AxisListType.X)
    vc(v.tensor_scalar_mul(t[:, 4:5], t[:, 4:5], 1.0 / 2048.0))
    v.scalar_tensor_tensor(t[:, 5:6], t[:, 3:4], 0.125, t[:, 4:5], ALU.mult, ALU.add)
    vc(v.tensor_mul(t[:, 6:7], t[:, 2:3], t[:, 2:3]))
    return v.tensor_sub(t[:, 5:6], t[:, 5:6], t[:, 6:7])


def build(debug=False, phases=5):
    nc = bass.Bass("TRN2", target_bir_lowering=False)
    es = ExitStack()

    dx = nc.dram_tensor("x_own", [TLOC, C], F32, kind="ExternalInput").ap()
    dxp = nc.dram_tensor("x_par", [TLOC, C], F32, kind="ExternalInput").ap()
    dwqkvT = nc.dram_tensor("wqkvT", [C, 3 * C], F32, kind="ExternalInput").ap()
    dwoT = nc.dram_tensor("woT", [C, C], F32, kind="ExternalInput").ap()
    dpkT = nc.dram_tensor("pkT", [D, R], F32, kind="ExternalInput").ap()
    dpvT = nc.dram_tensor("pvT", [D, R], F32, kind="ExternalInput").ap()
    duoT = nc.dram_tensor("uoT", [R, D], F32, kind="ExternalInput").ap()
    dw1T = nc.dram_tensor("w1T", [C, 512], F32, kind="ExternalInput").ap()
    dw2T = nc.dram_tensor("w2T", [128, 4], F32, kind="ExternalInput").ap()
    dwgT = nc.dram_tensor("wgT", [C, HID], I32, kind="ExternalInput").ap()
    dwuT = nc.dram_tensor("wuT", [C, HID], I32, kind="ExternalInput").ap()
    dwdT = nc.dram_tensor("wdT", [HID, C], I32, kind="ExternalInput").ap()
    dsg = nc.dram_tensor("sg2", [128, NHT], F32, kind="ExternalInput").ap()
    dsu = nc.dram_tensor("su2", [128, NHT], F32, kind="ExternalInput").ap()
    dsd = nc.dram_tensor("sd2", [128, NCT], F32, kind="ExternalInput").ap()
    dropeC = nc.dram_tensor("ropeC", [D, 4096], F32, kind="ExternalInput").ap()
    dropeS = nc.dram_tensor("ropeS", [D, 4096], F32, kind="ExternalInput").ap()
    dident = nc.dram_tensor("identB", [128, 128], BF, kind="ExternalInput").ap()
    didentF = nc.dram_tensor("identF", [128, 128], F32, kind="ExternalInput").ap()
    dmasklo = nc.dram_tensor("masklo", [128, 128], BF, kind="ExternalInput").ap()
    dmaskhi = nc.dram_tensor("maskhi", [128, 128], BF, kind="ExternalInput").ap()
    dbmask = nc.dram_tensor("bmask", [4, 128, 512], BF, kind="ExternalInput").ap()
    dcst = nc.dram_tensor("cst", [128, 8], F32, kind="ExternalInput").ap()
    donesrow = nc.dram_tensor("onesrow", [1, 128], BF, kind="ExternalInput").ap()

    dout = nc.dram_tensor("out", [TLOC, C], F32, kind="ExternalOutput").ap()
    scr_q = nc.dram_tensor("scr_q", [H, 128, TLOC], BF, kind="Internal").ap()
    scr_k = nc.dram_tensor("scr_k", [H, 128, 4096], BF, kind="Internal").ap()
    scr_vs = nc.dram_tensor("scr_vs", [H, 128, 2560], BF, kind="Internal").ap()
    scr_kr = nc.dram_tensor("scr_kr", [H, 64, 4096], BF, kind="Internal").ap()
    scr_qr = nc.dram_tensor("scr_qr", [H, 64, TLOC], BF, kind="Internal").ap()
    scr_vra = nc.dram_tensor("scr_vra", [H, 128, 32 * 65], BF, kind="Internal").ap()
    scr_x1 = nc.dram_tensor("scr_x1", [TLOC, C], F32, kind="Internal").ap()
    scr_h = nc.dram_tensor("scr_h", [NHT, 128, 1024], BF, kind="Internal").ap()
    scr_d = nc.dram_tensor("scr_d", [NCT, 128, 1024], BF, kind="Internal").ap()
    scr_wgbf = nc.dram_tensor("scr_wgbf", [NHT, 128, NCT * 128], BF, kind="Internal").ap()
    scr_wubf = nc.dram_tensor("scr_wubf", [NHT, 128, NCT * 128], BF, kind="Internal").ap()
    scr_wdbf = nc.dram_tensor("scr_wdbf", [NCT, 128, NHT * 128], BF, kind="Internal").ap()
    scr_wobf = nc.dram_tensor("scr_wobf", [NCT, 128, NCT * 128], BF, kind="Internal").ap()
    scr_xn2 = nc.dram_tensor("scr_xn2", [4, NCT, 128, CH], BF, kind="Internal").ap()

    dbg = {}
    if debug:
        def dbgt(name, shape, dt=BF):
            dbg[name] = nc.dram_tensor("dbg_" + name, shape, dt, kind="ExternalOutput").ap()
        dbgt("xnT", [NCT, 128, 4096])
        dbgt("gate", [1, TLOC], F32)
        dbgt("q", [H, 128, TLOC])
        dbgt("k", [H, 128, 4096])
        dbgt("vs", [H, 128, 2560])
        dbgt("kr", [H, 64, 4096])
        dbgt("qr", [H, 64, TLOC])
        dbgt("vra", [H, 128, 32 * 65])
        dbgt("ctx", [128, H * TLOC])
        dbgt("x1", [TLOC, C], F32)
        dbgt("xn2", [4, NCT, 128, CH])
        dbgt("h", [2, NHT, 128, 1024])
        dbgt("dn", [2, NCT, 128, 1024])

    esx = ExitStack()
    xnT = esx.enter_context(nc.sbuf_tensor("xnT", [128, NCT * 4096], BF))
    gate_row = es.enter_context(nc.sbuf_tensor("gate_row", [1, TLOC], F32, side="right"))
    cst = es.enter_context(nc.sbuf_tensor("cst_sb", [128, 8], F32, side="right"))
    identB = es.enter_context(nc.sbuf_tensor("identB_sb", [128, 128], BF, side="right"))
    identF = es.enter_context(nc.sbuf_tensor("identF_sb", [128, 128], F32, side="right"))
    onescol = es.enter_context(nc.sbuf_tensor("onescol", [128, 1], BF, side="right"))
    onesrow = es.enter_context(nc.sbuf_tensor("onesrow_sb", [1, 128], BF, side="right"))
    masklo = es.enter_context(nc.sbuf_tensor("masklo_sb", [128, 128], BF, side="right"))
    maskhi = es.enter_context(nc.sbuf_tensor("maskhi_sb", [128, 128], BF, side="right"))
    bmask = es.enter_context(nc.sbuf_tensor("bmask_sb", [128, 4 * 512], BF, side="right"))
    uoTb = es.enter_context(nc.sbuf_tensor("uoTb", [64, 128], BF, side="right"))
    pkb = es.enter_context(nc.sbuf_tensor("pkb", [128, R], BF, side="right"))
    pkb2 = es.enter_context(nc.sbuf_tensor("pkb2", [128, R], BF, side="right"))
    pvb = es.enter_context(nc.sbuf_tensor("pvb", [128, R], BF, side="right"))
    ropeC = esx.enter_context(nc.sbuf_tensor("ropeC_sb", [128, 4096], BF))
    ropeS = esx.enter_context(nc.sbuf_tensor("ropeS_sb", [128, 4096], BF))

    s_vc = Sem(es, nc, "s_vc")

    def vchain(inst):
        s_vc.n += 1
        inst.then_inc(s_vc.h, 1)
        nc.vector.wait_ge(s_vc.h, s_vc.n)
        return inst

    s_ld = Sem(es, nc, "s_ld")
    s_st = Sem(es, nc, "s_st")
    s_mm = Sem(es, nc, "s_mm")
    s_dv = Sem(es, nc, "s_dv")
    s_ac = Sem(es, nc, "s_ac")

    CST_SHIFT_L = cst[:, 0:1]
    CST_SHIFT_G = cst[:, 1:2]
    CST_EPS = cst[:, 2:3]

    blk = nc.Block()
    block = blk.__enter__()

    def barrier():
        for engdec in (block.sync, block.vector, block.scalar, block.tensor, block.gpsimd):
            @engdec
            def _(e):
                for s in (s_ld, s_st, s_mm, s_dv, s_ac):
                    if s.n > 0:
                        e.wait_ge(s.h, s.n)

    @block.sync
    def _(sync):
        sync.dma_start(cst[:], dcst[:]).then_inc(s_ld.h, 16)
        sync.dma_start(identB[:], dident[:]).then_inc(s_ld.h, 16)
        sync.dma_start(identF[:], didentF[:]).then_inc(s_ld.h, 16)
        sync.dma_start(masklo[:], dmasklo[:]).then_inc(s_ld.h, 16)
        sync.dma_start(maskhi[:], dmaskhi[:]).then_inc(s_ld.h, 16)
        sync.dma_start(bmask.ap().rearrange("p (j t) -> p j t", j=4),
                       dbmask.rearrange("j p t -> p j t")).then_inc(s_ld.h, 16)
        sync.dma_start(onesrow[:], donesrow[:]).then_inc(s_ld.h, 16)
    s_ld.n += 7 * 16

    @block.vector
    def _(v):
        _w(v, s_ld, s_ld.n)
        v.memset(onescol[:], 1.0).then_inc(s_dv.h, 1)
    s_dv.n += 1

    es0 = ExitStack()
    p0_xbuf = [es0.enter_context(nc.sbuf_tensor(f"p0x{i}", [128, C], F32)) for i in range(2)]
    p0_xn = [es0.enter_context(nc.sbuf_tensor(f"p0n{i}", [128, C], BF)) for i in range(2)]
    p0_st = es0.enter_context(nc.sbuf_tensor("p0st", [128, 24], F32))
    p0_tmp = es0.enter_context(nc.sbuf_tensor("p0tmp", [128, 16], F32))
    es0p = ExitStack()
    p0_ps = [es0p.enter_context(nc.psum_tensor(f"p0ps{i}", [128, 1024], BF)) for i in range(2)]
    p_info = es0p.enter_context(nc.psum_tensor("p_info", [128, 512], F32))
    p_z = es0p.enter_context(nc.psum_tensor("p_z", [128, 512], F32))

    xbuf_free = [0, 0]
    xn_free = [0, 0]
    ps_free = [0, 0]

    for half in range(2):
        src = dx if half == 0 else dxp
        for tt in range(16):
            bi = (half * 16 + tt) % 2
            tglob = half * 2048 + tt * 128

            @block.sync
            def _(sync, bi=bi, tt=tt, src=src):
                _w(sync, s_dv, xbuf_free[bi])
                sync.dma_start(p0_xbuf[bi][:], src[tt * 128:(tt + 1) * 128, :]).then_inc(s_ld.h, 16)
            s_ld.n += 16
            ld_done = s_ld.n

            @block.vector
            def _(v, bi=bi):
                _w(v, s_ld, ld_done)
                _ln_stats(v, p0_xbuf[bi], p0_st, p0_tmp, vchain).then_inc(s_dv.h, 1)
            s_dv.n += 1
            var_done = s_dv.n

            @block.scalar
            def _(s):
                _w(s, s_dv, var_done)
                s.activation(p0_tmp[:, 7:8], p0_tmp[:, 5:6], AF.Sqrt, bias=CST_EPS).then_inc(s_ac.h, 1)
            s_ac.n += 1
            sd_done = s_ac.n

            @block.vector
            def _(v, bi=bi):
                _w(v, s_ac, sd_done)
                _w(v, s_dv, xn_free[bi])
                vchain(v.reciprocal(p0_tmp[:, 6:7], p0_tmp[:, 7:8]))
                v.tensor_scalar(p0_xn[bi][:], p0_xbuf[bi][:], p0_tmp[:, 2:3],
                                p0_tmp[:, 6:7], ALU.subtract, ALU.mult).then_inc(s_dv.h, 1)
            s_dv.n += 1
            xn_done = s_dv.n
            xbuf_free[bi] = xn_done

            @block.tensor
            def _(tns, bi=bi):
                _w(tns, s_dv, xn_done)
                for grp in range(2):
                    _w(tns, s_dv, ps_free[grp])
                    for j in range(8):
                        ct = grp * 8 + j
                        i = tns.transpose(p0_ps[grp][:, j * 128:(j + 1) * 128],
                                          p0_xn[bi][:, ct * 128:(ct + 1) * 128], identB[:])
                    i.then_inc(s_mm.h, 1)
            s_mm.n += 2
            tr_done = s_mm.n

            @block.vector
            def _(v, tglob=tglob):
                _w(v, s_mm, tr_done - 1)
                for j in range(8):
                    ct = j
                    i = v.tensor_copy(xnT[:, ct * 4096 + tglob:ct * 4096 + tglob + 128],
                                      p0_ps[0][:, j * 128:(j + 1) * 128])
                i.then_inc(s_dv.h, 1)
            s_dv.n += 1
            ps_free[0] = s_dv.n

            @block.scalar
            def _(s, tglob=tglob):
                _w(s, s_mm, tr_done)
                for j in range(8):
                    ct = 8 + j
                    i = s.copy(xnT[:, ct * 4096 + tglob:ct * 4096 + tglob + 128],
                               p0_ps[1][:, j * 128:(j + 1) * 128])
                i.then_inc(s_dv.h, 1)
            s_dv.n += 1
            ps_free[1] = s_dv.n
            xn_free[bi] = s_dv.n

    es0i = ExitStack()
    w1b = es0i.enter_context(nc.sbuf_tensor("w1b", [128, NCT * 4 * 128], BF))
    w2b = es0i.enter_context(nc.sbuf_tensor("w2b", [128, 4], BF))
    hinfo = es0i.enter_context(nc.sbuf_tensor("hinfo", [128, 4 * 512], BF))
    sigz = es0i.enter_context(nc.sbuf_tensor("sigz", [1, 512], F32))

    w1cast_prev = 0
    for rt in range(4):
        @block.sync
        def _(sync, rt=rt, wp=w1cast_prev):
            if rt == 0:
                _w(sync, s_dv, max(xbuf_free))
            else:
                _w(sync, s_dv, wp)
            sync.dma_start(p0_xbuf[0].ap().rearrange("p (ct f) -> p ct f", ct=16),
                           dw1T[:, rt * 128:(rt + 1) * 128].rearrange("(ct p) f -> p ct f", p=128)
                           ).then_inc(s_ld.h, 16)
            if rt == 0:
                sync.dma_start(p0_xbuf[1][:, 0:4], dw2T[:]).then_inc(s_ld.h, 16)
        s_ld.n += 32 if rt == 0 else 16
        w1_ld = s_ld.n

        @block.vector
        def _(v, rt=rt):
            _w(v, s_ld, w1_ld)
            i = v.tensor_copy(w1b.ap().rearrange("p (ct rt f) -> p ct rt f", ct=16, rt=4)[:, :, rt, :],
                              p0_xbuf[0].ap().rearrange("p (ct f) -> p ct f", ct=16))
            if rt == 0:
                i = v.tensor_copy(w2b[:], p0_xbuf[1][:, 0:4])
            i.then_inc(s_dv.h, 1)
        s_dv.n += 1
        w1cast_prev = s_dv.n
    infow_cast = s_dv.n

    info_ps_free = 0
    z_ps_free = 0
    hinfo_free = 0
    sigz_read = 0
    for ch in range(4):
        for rt in range(4):
            @block.tensor
            def _(tns, ch=ch, rt=rt, ipf=info_ps_free):
                if ch == 0 and rt == 0:
                    _w(tns, s_dv, infow_cast)
                _w(tns, s_ac, ipf)
                for ct in range(16):
                    i = tns.matmul(p_info[:],
                                   w1b[:, (ct * 4 + rt) * 128:(ct * 4 + rt + 1) * 128],
                                   xnT[:, ct * 4096 + ch * 512:ct * 4096 + (ch + 1) * 512],
                                   start=(ct == 0), stop=(ct == 15))
                i.then_inc(s_mm.h, 1)
            s_mm.n += 1
            mm_done = s_mm.n

            @block.scalar
            def _(s, rt=rt, hf=hinfo_free):
                _w(s, s_mm, max(mm_done, hf))
                s.activation(hinfo[:, rt * 512:(rt + 1) * 512],
                             p_info[:], AF.Gelu).then_inc(s_ac.h, 1)
            s_ac.n += 1
            info_ps_free = s_ac.n
        gelu_done = s_ac.n

        @block.tensor
        def _(tns, ch=ch, zpf=z_ps_free, gd=gelu_done):
            _w(tns, s_ac, max(gd, zpf))
            for rt in range(4):
                i = tns.matmul(p_z[0:1, 0:512], w2b[:, rt:rt + 1],
                               hinfo[:, rt * 512:(rt + 1) * 512],
                               start=(rt == 0), stop=(rt == 3))
            i.then_inc(s_mm.h, 1)
        s_mm.n += 1
        z_done = s_mm.n
        hinfo_free = z_done

        @block.scalar
        def _(s, ch=ch, szr=sigz_read):
            _w(s, s_mm, z_done)
            _w(s, s_dv, szr)
            s.activation(sigz[0:1, 0:512], p_z[0:1, 0:512],
                         AF.Sigmoid).then_inc(s_ac.h, 1)
        s_ac.n += 1
        sig_done = s_ac.n
        z_ps_free = sig_done

        @block.vector
        def _(v, ch=ch):
            _w(v, s_ac, sig_done)
            v.tensor_single_scalar(gate_row[0:1, ch * 512:(ch + 1) * 512],
                                   sigz[0:1, 0:512], 0.75,
                                   ALU.is_gt).then_inc(s_dv.h, 1)
        s_dv.n += 1
        sigz_read = s_dv.n

    @block.sync
    def _(sync):
        _w(sync, s_dv, s_dv.n)
        sync.dma_start(p0_xbuf[0][:], dropeC[:, 0:2048]).then_inc(s_ld.h, 16)
        sync.dma_start(p0_xbuf[1][:], dropeC[:, 2048:4096]).then_inc(s_ld.h, 16)
    s_ld.n += 32
    rc_ld = s_ld.n

    @block.vector
    def _(v):
        _w(v, s_ld, rc_ld)
        v.tensor_copy(ropeC[:, 0:2048], p0_xbuf[0][:])
        v.tensor_copy(ropeC[:, 2048:4096], p0_xbuf[1][:]).then_inc(s_dv.h, 1)
    s_dv.n += 1
    rc_cast = s_dv.n

    @block.sync
    def _(sync):
        _w(sync, s_dv, rc_cast)
        sync.dma_start(p0_xbuf[0][:], dropeS[:, 0:2048]).then_inc(s_ld.h, 16)
        sync.dma_start(p0_xbuf[1][:], dropeS[:, 2048:4096]).then_inc(s_ld.h, 16)
    s_ld.n += 32
    rs_ld = s_ld.n

    @block.vector
    def _(v):
        _w(v, s_ld, rs_ld)
        v.tensor_copy(ropeS[:, 0:2048], p0_xbuf[0][:])
        v.tensor_copy(ropeS[:, 2048:4096], p0_xbuf[1][:]).then_inc(s_dv.h, 1)
    s_dv.n += 1
    rs_cast = s_dv.n

    @block.sync
    def _(sync):
        _w(sync, s_dv, rs_cast)
        sync.dma_start(p0_xbuf[0][:, 0:R], dpkT[:]).then_inc(s_ld.h, 16)
        sync.dma_start(p0_xbuf[0][:, R:2 * R], dpvT[:]).then_inc(s_ld.h, 16)
        sync.dma_start(p0_xbuf[0][0:64, 2 * R:2 * R + 128], duoT[:]).then_inc(s_ld.h, 16)
    s_ld.n += 48
    pk_ld = s_ld.n

    @block.vector
    def _(v):
        _w(v, s_ld, pk_ld)
        v.tensor_copy(pkb[:], p0_xbuf[0][:, 0:R])
        v.tensor_scalar(pkb2[:], p0_xbuf[0][:, 0:R], float(D) / float(R), None, ALU.mult)
        v.tensor_copy(pvb[:], p0_xbuf[0][:, R:2 * R])
        v.tensor_copy(uoTb[:], p0_xbuf[0][0:64, 2 * R:2 * R + 128]).then_inc(s_dv.h, 1)
    s_dv.n += 1

    if debug:
        @block.gpsimd
        def _(g):
            _w(g, s_dv, s_dv.n)
            g.dma_start(dbg["xnT"].rearrange("ct p t -> p ct t"),
                        xnT.ap().rearrange("p (ct t) -> p ct t", ct=NCT)).then_inc(s_st.h, 16)
            g.dma_start(dbg["gate"][:], gate_row[:]).then_inc(s_st.h, 16)
        s_st.n += 32

    barrier()
    es0i.close()
    es0.close()
    es0p.close()

    if phases <= 0:
        return _finish(nc, blk, block, es, (s_ld, s_st, s_mm, s_dv, s_ac)), dbg

    es1 = ExitStack()
    wld = es1.enter_context(nc.sbuf_tensor("p1w", [128, NCT * 128], F32))
    wbf = es1.enter_context(nc.sbuf_tensor("p1wb", [128, NCT * 128], BF))
    stage = [es1.enter_context(nc.sbuf_tensor(f"p1s{i}", [128, 4096], BF)) for i in range(2)]
    rtmp = es1.enter_context(nc.sbuf_tensor("p1rt", [128, 3 * CH], BF))
    rstage = es1.enter_context(nc.sbuf_tensor("p1r", [128, 4096], BF))
    vsst = es1.enter_context(nc.sbuf_tensor("p1vs", [128, 2560], BF))
    es1p = ExitStack()
    p1_pa = [es1p.enter_context(nc.psum_tensor(f"p1pa{i}", [128, 512], F32)) for i in range(2)]
    p1_pr = [es1p.enter_context(nc.psum_tensor(f"p1pr{i}", [128, 512], F32)) for i in range(2)]
    p1_pt = [es1p.enter_context(nc.psum_tensor(f"p1pt{i}", [128, 1024], BF)) for i in range(2)]

    wld_free = 0
    wbf_free = 0
    stage_free = [0, 0]
    rstage_free = 0
    vsst_free = 0
    pa_free = [0, 0]
    pr_free = [0, 0]
    pt_free = [0, 0]

    BAND_COLS = list(range(0, 2048, 128)) + list(range(3584, 4096, 128))

    for ft in range(48):
        fb = ft % 2
        kind = ft // 16
        head = ft % 16
        chunks = range(4) if kind == 0 else range(8)

        @block.sync
        def _(sync, ft=ft, wf=wld_free):
            _w(sync, s_dv, wf)
            sync.dma_start(wld.ap().rearrange("p (ct f) -> p ct f", ct=NCT),
                           dwqkvT[:, ft * 128:(ft + 1) * 128].rearrange("(ct p) f -> p ct f", p=128)
                           ).then_inc(s_ld.h, 16)
        s_ld.n += 16
        w_ld = s_ld.n

        @block.vector
        def _(v, wbff=wbf_free, kind=kind):
            _w(v, s_ld, w_ld)
            _w(v, s_mm, wbff)
            if kind == 0:
                v.tensor_scalar(wbf[:], wld[:], SCALE_L, None, ALU.mult).then_inc(s_dv.h, 1)
            else:
                v.tensor_copy(wbf[:], wld[:]).then_inc(s_dv.h, 1)
        s_dv.n += 1
        w_cast = s_dv.n
        wld_free = w_cast

        for ci, ch in enumerate(chunks):
            pb = ci % 2

            @block.tensor
            def _(tns, fb=fb, ch=ch, pb=pb, ci=ci, paf=pa_free[pb], sf=stage_free[fb]):
                if ci == 0:
                    _w(tns, s_dv, w_cast)
                    _w(tns, s_st, sf)
                _w(tns, s_dv, paf)
                for ct in range(NCT):
                    i = tns.matmul(p1_pa[pb][:], wbf[:, ct * 128:(ct + 1) * 128],
                                   xnT[:, ct * 4096 + ch * 512:ct * 4096 + (ch + 1) * 512],
                                   start=(ct == 0), stop=(ct == NCT - 1))
                i.then_inc(s_mm.h, 1)
            s_mm.n += 1
            mm_done = s_mm.n
            if ci == len(chunks) - 1:
                wbf_free = mm_done

            col0 = ch * 512
            if kind == 2:
                @block.vector
                def _(v, fb=fb, pb=pb, col0=col0):
                    _w(v, s_mm, mm_done)
                    v.tensor_copy(stage[fb][:, col0:col0 + 512], p1_pa[pb][:]).then_inc(s_dv.h, 1)
                s_dv.n += 1
                pa_free[pb] = s_dv.n
            else:
                copy_cnt = s_dv.n + 1
                @block.vector
                def _(v, fb=fb, pb=pb, col0=col0, cc=copy_cnt):
                    _w(v, s_mm, mm_done)
                    st = stage[fb][:, col0:col0 + 512]
                    v.tensor_copy(st, p1_pa[pb][:]).then_inc(s_dv.h, 1)
                    v.wait_ge(s_dv.h, cc)
                    v.stream_shuffle(rtmp[:, 0:512], st, PAIRSWAP)
                    vchain(v.tensor_mul(rtmp[:, 512:1024], st, ropeC[:, col0:col0 + 512]))
                    vchain(v.tensor_mul(rtmp[:, 1024:1536], rtmp[:, 0:512], ropeS[:, col0:col0 + 512]))
                    v.tensor_add(st, rtmp[:, 512:1024], rtmp[:, 1024:1536]).then_inc(s_dv.h, 1)
                s_dv.n += 2
                pa_free[pb] = s_dv.n - 1
        stage_done = s_dv.n

        if kind in (0, 1):
            nj = 4 if kind == 0 else 8
            for j in range(nj):
                @block.tensor
                def _(tns, fb=fb, j=j, prf=pr_free[j % 2], kind=kind):
                    if j == 0:
                        _w(tns, s_dv, stage_done)
                    _w(tns, s_dv, prf)
                    tns.matmul(p1_pr[j % 2][0:64, 0:512],
                               (pkb if kind == 0 else pkb2)[:, 0:64],
                               stage[fb][:, j * 512:(j + 1) * 512],
                               start=True, stop=True).then_inc(s_mm.h, 1)
                s_mm.n += 1
                prmm = s_mm.n

                @block.vector
                def _(v, j=j, prmm=prmm, rf=rstage_free):
                    _w(v, s_mm, prmm)
                    if j == 0:
                        _w(v, s_st, rf)
                    v.tensor_copy(rstage[0:64, j * 512:(j + 1) * 512],
                                  p1_pr[j % 2][0:64, 0:512]).then_inc(s_dv.h, 1)
                s_dv.n += 1
                pr_free[j % 2] = s_dv.n
            proj_done = s_dv.n

            @block.gpsimd
            def _(g, fb=fb, head=head, kind=kind):
                _w(g, s_dv, proj_done)
                if kind == 0:
                    g.dma_start(scr_q[head], stage[fb][:, 0:2048]).then_inc(s_st.h, 16)
                    g.dma_start(scr_qr[head], rstage[0:64, 0:2048]).then_inc(s_st.h, 16)
                    if debug:
                        g.dma_start(dbg["q"][head], stage[fb][:, 0:2048]).then_inc(s_st.h, 16)
                        g.dma_start(dbg["qr"][head], rstage[0:64, 0:2048]).then_inc(s_st.h, 16)
                else:
                    g.dma_start(scr_k[head], stage[fb][:]).then_inc(s_st.h, 16)
                    g.dma_start(scr_kr[head], rstage[0:64, 0:4096]).then_inc(s_st.h, 16)
                    if debug:
                        g.dma_start(dbg["k"][head], stage[fb][:]).then_inc(s_st.h, 16)
                        g.dma_start(dbg["kr"][head], rstage[0:64, 0:4096]).then_inc(s_st.h, 16)
            s_st.n += 64 if debug else 32
            stage_free[fb] = s_st.n
            rstage_free = s_st.n

        else:
            @block.vector
            def _(v, rf=rstage_free):
                _w(v, s_st, rf)
                v.memset(rstage[:], 1.0).then_inc(s_dv.h, 1)
            s_dv.n += 1
            vra_init = s_dv.n

            for gi in range(4):
                slot = gi % 2

                @block.tensor
                def _(tns, fb=fb, gi=gi, slot=slot, prf=pr_free[slot]):
                    if gi == 0:
                        _w(tns, s_dv, stage_done)
                    _w(tns, s_dv, prf)
                    for j in range(8):
                        si = gi * 8 + j
                        i = tns.matmul(p1_pr[slot][:, j * 64:(j + 1) * 64],
                                       stage[fb][:, si * 128:(si + 1) * 128],
                                       pvb[:, 0:64], start=True, stop=True)
                    i.then_inc(s_mm.h, 1)
                s_mm.n += 1
                prmm = s_mm.n

                @block.vector
                def _(v, gi=gi, slot=slot, prmm=prmm):
                    _w(v, s_mm, prmm)
                    if gi == 0:
                        _w(v, s_dv, vra_init)
                    v.tensor_copy(
                        rstage.ap()[:, gi * 520:(gi + 1) * 520].rearrange(
                            "p (j f) -> p j f", f=65)[:, :, 0:64],
                        p1_pr[slot].ap().rearrange("p (j f) -> p j f", f=64)
                    ).then_inc(s_dv.h, 1)
                s_dv.n += 1
                pr_free[slot] = s_dv.n
            proj_done = s_dv.n

            grp_sizes = [8, 8, 4]
            for gi, gs in enumerate(grp_sizes):
                gslot = gi % 2
                @block.tensor
                def _(tns, fb=fb, gi=gi, gs=gs, ptf=pt_free[gslot], gslot=gslot):
                    _w(tns, s_dv, ptf)
                    for j in range(gs):
                        scol = BAND_COLS[gi * 8 + j]
                        i = tns.transpose(p1_pt[gslot][:, j * 128:(j + 1) * 128],
                                          stage[fb][:, scol:scol + 128], identB[:])
                    i.then_inc(s_mm.h, 1)
                s_mm.n += 1
                trv = s_mm.n

                @block.vector
                def _(v, gi=gi, gs=gs, vf=vsst_free, trv=trv, gslot=gslot):
                    _w(v, s_mm, trv)
                    if gi == 0:
                        _w(v, s_st, vf)
                    for j in range(gs):
                        jj = gi * 8 + j
                        i = v.tensor_copy(vsst[:, jj * 128:(jj + 1) * 128],
                                          p1_pt[gslot][:, j * 128:(j + 1) * 128])
                    i.then_inc(s_dv.h, 1)
                s_dv.n += 1
                pt_free[gslot] = s_dv.n
            vs_done = s_dv.n

            @block.gpsimd
            def _(g, fb=fb, head=head):
                _w(g, s_dv, vs_done)
                g.dma_start(scr_vs[head], vsst[:]).then_inc(s_st.h, 16)
                g.dma_start(scr_vra[head], rstage[:, 0:32 * 65]).then_inc(s_st.h, 16)
                if debug:
                    g.dma_start(dbg["vs"][head], vsst[:]).then_inc(s_st.h, 16)
                    g.dma_start(dbg["vra"][head], rstage[:, 0:32 * 65]).then_inc(s_st.h, 16)
            s_st.n += 64 if debug else 32
            stage_free[fb] = s_st.n
            rstage_free = s_st.n
            vsst_free = s_st.n

    barrier()
    es1.close()
    es1p.close()
    esx.close()

    if phases <= 1:
        return _finish(nc, blk, block, es, (s_ld, s_st, s_mm, s_dv, s_ac)), dbg

    es_c = ExitStack()
    ctx_all = es_c.enter_context(nc.sbuf_tensor("ctx_all", [128, H * TLOC], BF))
    es2 = ExitStack()
    qb2 = [es2.enter_context(nc.sbuf_tensor(f"qb{i}", [128, TLOC], BF)) for i in range(2)]
    kb2 = [es2.enter_context(nc.sbuf_tensor(f"kb{i}", [128, 2560], BF)) for i in range(2)]
    vsb2 = [es2.enter_context(nc.sbuf_tensor(f"vsb{i}", [128, 2560], BF)) for i in range(2)]
    krb2 = [es2.enter_context(nc.sbuf_tensor(f"krb{i}", [128, 4096], BF)) for i in range(2)]
    qrb2 = [es2.enter_context(nc.sbuf_tensor(f"qrb{i}", [128, TLOC], BF)) for i in range(2)]
    vrab2 = [es2.enter_context(nc.sbuf_tensor(f"vrab{i}", [128, 32 * 65], BF)) for i in range(2)]
    attl = [es2.enter_context(nc.sbuf_tensor(f"attl{i}", [128, 512], BF)) for i in range(3)]
    attg = [es2.enter_context(nc.sbuf_tensor(f"attg{i}", [128, 1024], BF)) for i in range(3)]
    rowf = es2.enter_context(nc.sbuf_tensor("rowf", [1, 512], F32))
    rowf2 = es2.enter_context(nc.sbuf_tensor("rowf2", [1, 512], F32))
    rowg = es2.enter_context(nc.sbuf_tensor("rowg", [1, 512], F32))
    rowgb = es2.enter_context(nc.sbuf_tensor("rowgb", [1, 512], BF))
    rowb = es2.enter_context(nc.sbuf_tensor("rowb", [1, 512], BF))
    bcl = es2.enter_context(nc.sbuf_tensor("bcl", [128, 512], F32))
    bcg = es2.enter_context(nc.sbuf_tensor("bcg", [128, 512], F32))
    cgn = es2.enter_context(nc.sbuf_tensor("cgn", [64, 512], BF))
    ctmp = es2.enter_context(nc.sbuf_tensor("ctmp", [128, 512], F32))
    es2p = ExitStack()
    P_sc = [es2p.enter_context(nc.psum_tensor(f"Psc{i}", [128, 1024], F32)) for i in range(2)]
    P_ctx = es2p.enter_context(nc.psum_tensor("P_ctx", [128, 512], F32))
    P_rsbc = es2p.enter_context(nc.psum_tensor("P_rsbc", [128, 512], F32))
    P_cg = es2p.enter_context(nc.psum_tensor("P_cg", [128, 512], F32))
    P_G = es2p.enter_context(nc.psum_tensor("P_G", [128, 512], F32))

    def sc_slot(idx):
        i = idx % 4
        return P_sc[i // 2][:, (i % 2) * 512:(i % 2 + 1) * 512]

    hload_free = [0, 0]
    sc_free = [(None, 0)] * 4
    attl_free = [0, 0, 0]
    attg_free = [0, 0, 0]
    ctx_ps_free = 0
    rsbc_ps_free = 0
    cg_ps_free = 0
    G_ps_free = 0

    def band_pairs(tc):
        t0c = tc * 512
        pairs = []
        for s0 in range(max(0, t0c - 512), min(t0c + 512, 2048), 128):
            pairs.append(("own", s0, t0c - s0))
        if tc == 0:
            for j in range(4):
                sv0 = -512 + j * 128
                pairs.append(("tail", sv0, -sv0))
        return pairs

    LAG = 2
    DF = Defer(block)

    pw_int = [es2.enter_context(nc.sbuf_tensor(f"pwi{i}", [128, NCT * 128], I32)) for i in range(2)]
    pw_bf = [es2.enter_context(nc.sbuf_tensor(f"pwb{i}", [128, NCT * 128], BF)) for i in range(2)]
    pw_f32 = [pw_int[i].ap().bitcast(F32) for i in range(2)]
    s_pc = Sem(es, nc, "s_pc")
    pw_cols = ([("g", j, 0) for j in range(NHT)] + [("u", j, 0) for j in range(NHT)] +
               [("d", j, q) for j in range(NCT) for q in range(4)] +
               [("o", j, 0) for j in range(NCT)])
    pw_state = {"i": 0, "ld_free": [0, 0], "st_free": [0, 0]}

    def emit_precast(nmax):
        for _ in range(nmax):
            if pw_state["i"] >= len(pw_cols):
                return
            j = pw_state["i"]
            pw_state["i"] += 1
            kind4, col, q = pw_cols[j]
            buf = j % 2
            ncols = 11 * 128 if kind4 == "d" else NCT * 128

            def _pw_ld(sync, kind4=kind4, col=col, q=q, buf=buf, lf=pw_state["ld_free"][buf]):
                _w(sync, s_dv, lf)
                if kind4 == "g":
                    sync.dma_start(pw_int[buf].ap().rearrange("p (ct f) -> p ct f", ct=NCT),
                                   dwgT[:, col * 128:(col + 1) * 128].rearrange("(ct p) f -> p ct f", p=128)
                                   ).then_inc(s_pc.h, 16)
                elif kind4 == "u":
                    sync.dma_start(pw_int[buf].ap().rearrange("p (ct f) -> p ct f", ct=NCT),
                                   dwuT[:, col * 128:(col + 1) * 128].rearrange("(ct p) f -> p ct f", p=128)
                                   ).then_inc(s_pc.h, 16)
                elif kind4 == "d":
                    sync.dma_start(pw_int[buf].ap().rearrange("p (ct f) -> p ct f", ct=NCT)[:, 0:11, :],
                                   dwdT[q * 11 * 128:(q + 1) * 11 * 128, col * 128:(col + 1) * 128]
                                   .rearrange("(ht p) f -> p ht f", p=128)
                                   ).then_inc(s_pc.h, 16)
                else:
                    sync.dma_start(pw_f32[buf].rearrange("p (ct f) -> p ct f", ct=NCT),
                                   dwoT[:, col * 128:(col + 1) * 128].rearrange("(ct p) f -> p ct f", p=128)
                                   ).then_inc(s_pc.h, 16)
            DF.y(_pw_ld)
            s_pc.n += 16
            pw_ld_done = s_pc.n

            def _pw_cast(v, kind4=kind4, buf=buf, ncols=ncols, pld=pw_ld_done,
                         sf=pw_state["st_free"][buf]):
                _w(v, s_pc, pld)
                _w(v, s_st, sf)
                if kind4 == "o":
                    v.tensor_copy(pw_bf[buf][:, 0:ncols], pw_f32[buf][:, 0:ncols]).then_inc(s_dv.h, 1)
                else:
                    v.tensor_scalar(pw_bf[buf][:, 0:ncols], pw_int[buf][:, 0:ncols],
                                    -128.0, None, ALU.add).then_inc(s_dv.h, 1)
            DF.v(_pw_cast)
            s_dv.n += 1
            pw_state["ld_free"][buf] = s_dv.n
            pw_cast_done = s_dv.n

            def _pw_st(g, kind4=kind4, col=col, q=q, buf=buf, ncols=ncols, pcd=pw_cast_done):
                _w(g, s_dv, pcd)
                if kind4 == "d":
                    g.dma_start(scr_wdbf[col][:, q * 1408:(q + 1) * 1408],
                                pw_bf[buf][:, 0:ncols]).then_inc(s_st.h, 16)
                else:
                    dst = {"g": scr_wgbf, "u": scr_wubf, "o": scr_wobf}[kind4]
                    g.dma_start(dst[col], pw_bf[buf][:, 0:ncols]).then_inc(s_st.h, 16)
            DF.g(_pw_st)
            s_st.n += 16
            pw_state["st_free"][buf] = s_st.n
    tail_q = []
    tail_state = {}
    frees = {"ctx": 0, "rsbc": 0, "cg": 0, "cg_ac": 0, "G": 0}

    for h in range(H):
        hb = h % 2

        def _hload(sync, h=h, hb=hb, hf=hload_free[hb]):
            _w(sync, s_mm, hf)
            sync.dma_start(qb2[hb][:], scr_q[h]).then_inc(s_ld.h, 16)
            sync.dma_start(kb2[hb][:, 0:2048], scr_k[h][:, 0:2048]).then_inc(s_ld.h, 16)
            sync.dma_start(kb2[hb][:, 2048:2560], scr_k[h][:, 3584:4096]).then_inc(s_ld.h, 16)
            sync.dma_start(vsb2[hb][:], scr_vs[h]).then_inc(s_ld.h, 16)
            sync.dma_start(krb2[hb][0:64, :], scr_kr[h]).then_inc(s_ld.h, 16)
            sync.dma_start(krb2[hb][64:128, :], scr_kr[h]).then_inc(s_ld.h, 16)
            sync.dma_start(qrb2[hb][0:64, :], scr_qr[h]).then_inc(s_ld.h, 16)
            sync.dma_start(qrb2[hb][64:128, :], scr_qr[h]).then_inc(s_ld.h, 16)
            sync.dma_start(vrab2[hb][:], scr_vra[h]).then_inc(s_ld.h, 16)
        DF.y(_hload)
        s_ld.n += 9 * 16
        h_ld = s_ld.n

        for tc in range(4):
            t0c = tc * 512
            pairs = band_pairs(tc)
            n_l = len(pairs)
            n_items = n_l + 16

            exp_ready = [None] * n_items
            ab_of = [None] * n_items
            slot_of = [None] * n_items
            geo = [None] * n_items
            slot_box = [0]
            pending = list(tail_q)
            tail_q.clear()

            def pop_pending(pending=pending):
                if pending:
                    pending.pop(0)()

            def emit_scores(i, hb=hb, t0c=t0c, pairs=pairs, n_l=n_l,
                            geo=geo, slot_of=slot_of, slot_box=slot_box, h_ld=h_ld):
                if i < n_l:
                    pk_kind, s0, off = pairs[i]
                    if pk_kind == "own":
                        kbcol = s0
                        vcol = s0
                        tstart = max(s0, t0c)
                    else:
                        kbcol = 2048 + (s0 + 512)
                        vcol = kbcol
                        tstart = t0c
                    tend = min(s0 + 638, t0c + 511)
                    w = tend - tstart + 1
                    geo[i] = (pk_kind, s0, off, vcol, tstart, w)
                    slot = slot_box[0] % 4
                    slot_box[0] += 1
                    slot_of[i] = slot

                    def _df_tensor_1(tns, i=i, hb=hb, kbcol=kbcol, tstart=tstart, w=w, slot=slot,
                          scf=sc_free[slot], hl=h_ld):
                        if i == 0:
                            _w(tns, s_ld, hl)
                        if scf[0] is not None:
                            _w(tns, scf[0], scf[1])
                        tns.matmul(sc_slot(slot)[:, 0:w],
                                   kb2[hb][:, kbcol:kbcol + 128], qb2[hb][:, tstart:tstart + w],
                                   start=True, stop=True).then_inc(s_mm.h, 1)
                    DF.t(_df_tensor_1)
                    s_mm.n += 1
                else:
                    sgp = i - n_l
                    if slot_box[0] % 2:
                        slot_box[0] += 1
                    slot2 = slot_box[0] % 4
                    slot3 = (slot_box[0] + 1) % 4
                    slot_box[0] += 2
                    slot_of[i] = slot2

                    def _df_tensor_2(tns, hb=hb, t0c=t0c, sgp=sgp, slot2=slot2, slot3=slot3,
                          sf2=sc_free[slot2], sf3=sc_free[slot3]):
                        for sf in (sf2, sf3):
                            if sf[0] is not None:
                                _w(tns, sf[0], sf[1])
                        tns.matmul(sc_slot(slot2), krb2[hb][0:64, (2 * sgp) * 128:(2 * sgp + 1) * 128],
                                   qrb2[hb][0:64, t0c:t0c + 512], start=True, stop=True,
                                   tile_position=(0, 0))
                        tns.matmul(sc_slot(slot3), krb2[hb][64:128, (2 * sgp + 1) * 128:(2 * sgp + 2) * 128],
                                   qrb2[hb][64:128, t0c:t0c + 512], start=True,
                                   stop=True, tile_position=(64, 0)).then_inc(s_mm.h, 1)
                    DF.t(_df_tensor_2)
                    s_mm.n += 1
                return s_mm.n

            def emit_exp(i, sc_done, n_l=n_l, geo=geo, slot_of=slot_of,
                         ab_of=ab_of, exp_ready=exp_ready):
                if i < n_l:
                    (pk_kind, s0, off, vcol, tstart, w) = geo[i]
                    ai = i % 3
                    ab_of[i] = ai

                    def _df_scalar_1(s, w=w, ai=ai, af=attl_free[ai], sd=sc_done, slot=slot_of[i]):
                        _w(s, s_mm, max(sd, af))
                        s.activation(attl[ai][:, 0:w], sc_slot(slot)[:, 0:w], AF.Exp,
                                     bias=CST_SHIFT_L).then_inc(s_ac.h, 1)
                    DF.s(_df_scalar_1)
                    s_ac.n += 1
                    exp_done = s_ac.n
                    sc_free[slot_of[i]] = (s_ac, exp_done)

                    nmask = 0
                    if pk_kind == "own" and off <= 0:
                        def _df_vector_1(v, ai=ai, w=w, ed=exp_done):
                            _w(v, s_ac, ed)
                            mw = min(128, w)
                            v.tensor_mul(attl[ai][:, 0:mw], attl[ai][:, 0:mw],
                                         masklo[:, 0:mw]).then_inc(s_dv.h, 1)
                        DF.v(_df_vector_1)
                        s_dv.n += 1
                        nmask = 1
                    elif pk_kind == "own" and off >= 128:
                        colhi = 512 - off
                        def _df_vector_2(v, ai=ai, w=w, colhi=colhi, ed=exp_done):
                            _w(v, s_ac, ed)
                            mw = w - colhi
                            v.tensor_mul(attl[ai][:, colhi:colhi + mw], attl[ai][:, colhi:colhi + mw],
                                         maskhi[:, 0:mw]).then_inc(s_dv.h, 1)
                        DF.v(_df_vector_2)
                        s_dv.n += 1
                        nmask = 1
                    elif pk_kind == "tail":
                        j = (s0 + 512) // 128
                        def _df_vector_3(v, ai=ai, w=w, j=j, ed=exp_done):
                            _w(v, s_ac, ed)
                            v.tensor_mul(attl[ai][:, 0:w], attl[ai][:, 0:w],
                                         bmask[:, j * 512:j * 512 + w]).then_inc(s_dv.h, 1)
                        DF.v(_df_vector_3)
                        s_dv.n += 1
                        nmask = 1
                    exp_ready[i] = (s_dv, s_dv.n) if nmask else (s_ac, exp_done)
                else:
                    gi = i % 3
                    ab_of[i] = gi
                    slot2 = slot_of[i]

                    def _df_scalar_2(s, slot2=slot2, gi=gi, af=attg_free[gi], sd=sc_done):
                        _w(s, s_mm, max(sd, af))
                        s.activation(attg[gi][:], P_sc[slot2 // 2][:], AF.Exp,
                                     bias=CST_SHIFT_L).then_inc(s_ac.h, 1)
                    DF.s(_df_scalar_2)
                    s_ac.n += 1
                    exp_done = s_ac.n
                    sc_free[slot2] = (s_ac, exp_done)
                    sc_free[(slot2 + 1) % 4] = (s_ac, exp_done)
                    exp_ready[i] = (s_ac, exp_done)

            def emit_ctx(i, hb=hb, t0c=t0c, n_l=n_l, geo=geo, ab_of=ab_of,
                         exp_ready=exp_ready):
                sem, cnt = exp_ready[i]
                ai = ab_of[i]
                if i < n_l:
                    (pk_kind, s0, off, vcol, tstart, w) = geo[i]

                    def _df_tensor_3(tns, i=i, hb=hb, vcol=vcol, tstart=tstart, t0c=t0c, w=w, ai=ai,
                          sem=sem, cnt=cnt, n_l=n_l,
                          cpf=frees["ctx"], rbf=frees["rsbc"]):
                        if i == 0:
                            _w(tns, s_dv, cpf)
                            _w(tns, s_ac, rbf)
                        _w(tns, sem, cnt)
                        trel = tstart - t0c
                        tns.matmul(P_ctx[:, trel:trel + w], vsb2[hb][:, vcol:vcol + 128],
                                   attl[ai][:, 0:w], start=(i == 0), stop=(i == n_l - 1))
                        tns.matmul(P_rsbc[0:1, trel:trel + w], onescol[:, 0:1],
                                   attl[ai][:, 0:w], start=(i == 0),
                                   stop=(i == n_l - 1)).then_inc(s_mm.h, 1)
                    DF.t(_df_tensor_3)
                    s_mm.n += 1
                    attl_free[ai] = s_mm.n
                else:
                    sgp = i - n_l

                    def _df_tensor_4(tns, hb=hb, sgp=sgp, ai=ai, sem=sem, cnt=cnt,
                          cgf=frees["cg"], cga=frees["cg_ac"]):
                        if sgp == 0:
                            _w(tns, s_dv, cgf)
                            _w(tns, s_ac, cga)
                        _w(tns, sem, cnt)
                        tns.matmul(P_cg[0:65, 0:512], vrab2[hb][:, (2 * sgp) * 65:(2 * sgp + 1) * 65],
                                   attg[ai][:, 0:512], start=(sgp == 0), stop=False)
                        tns.matmul(P_cg[0:65, 0:512], vrab2[hb][:, (2 * sgp + 1) * 65:(2 * sgp + 2) * 65],
                                   attg[ai][:, 512:1024], start=False,
                                   stop=(sgp == 15)).then_inc(s_mm.h, 1)
                    DF.t(_df_tensor_4)
                    s_mm.n += 1
                    attg_free[ai] = s_mm.n

            for i in range(n_items):
                sc_done = emit_scores(i)
                emit_exp(i, sc_done)
                if i in (0, 5, 10):
                    pop_pending()
                if i >= LAG:
                    emit_ctx(i - LAG)
            for j in range(max(0, n_items - LAG), n_items):
                emit_ctx(j)
            while pending:
                pop_pending()

            lmd_rs = s_mm.n

            def _df_scalar_recips(s, lmd=lmd_rs):
                _w(s, s_mm, lmd)
                s.activation(rowf[0:1, 0:512], P_rsbc[0:1, 0:512], AF.Ln)
                s.activation(rowf2[0:1, 0:512], P_cg[64:65, 0:512], AF.Ln)
                s.activation(rowb[0:1, 0:512], rowf[0:1, 0:512],
                             AF.Exp, scale=-1.0).then_inc(s_ac.h, 1)
                s.activation(rowg[0:1, 0:512], rowf2[0:1, 0:512],
                             AF.Exp, scale=-1.0).then_inc(s_ac.h, 1)
            DF.s(_df_scalar_recips)
            s_ac.n += 2
            recipl_cnt = s_ac.n - 1
            recipg_cnt = s_ac.n
            frees["rsbc"] = recipl_cnt
            frees["cg_ac"] = recipg_cnt

            emit_precast(3)
            invl_done = s_dv.n
            cg_mm_done = s_mm.n
            hload_free[hb] = cg_mm_done

            def _df_vector_invg(v, tc=tc, rg=recipg_cnt):
                _w(v, s_ac, rg)
                v.tensor_mul(rowgb[:], rowg[:],
                             gate_row[0:1, tc * 512:(tc + 1) * 512]).then_inc(s_dv.h, 1)
            DF.v(_df_vector_invg)
            s_dv.n += 1
            invg_done = s_dv.n
            if tc % 2 == 1:
                DF.flush()

            def t1(invl_done=invl_done, rb=recipl_cnt, hb=hb, h=h, tc=tc):
                slot = slot_box[0] % 4
                slot_box[0] += 1
                sf = sc_free[slot]

                def _t1_mm(tns, slot=slot, sf=sf, invl_done=invl_done, rb=rb):
                    if sf[0] is not None:
                        _w(tns, sf[0], sf[1])
                    _w(tns, s_dv, invl_done)
                    _w(tns, s_ac, rb)
                    tns.matmul(sc_slot(slot), onesrow[0:1, :], rowb[0:1, :],
                               start=True, stop=True).then_inc(s_mm.h, 1)
                DF.t(_t1_mm)
                s_mm.n += 1
                bcl_mm = s_mm.n

                base1 = s_dv.n

                def _t1_v(v, slot=slot, bcl_mm=bcl_mm, base=base1):
                    _w(v, s_mm, bcl_mm)
                    v.tensor_copy(bcl[:], sc_slot(slot)).then_inc(s_dv.h, 1)
                    v.wait_ge(s_dv.h, base + 1)
                    v.tensor_mul(ctmp[:], P_ctx[:, 0:512], bcl[:]).then_inc(s_dv.h, 1)
                DF.v(_t1_v)
                s_dv.n += 2
                sc_free[slot] = (s_dv, s_dv.n - 1)
                frees["ctx"] = s_dv.n
                tail_state["ctmp"] = s_dv.n

            def t2(hb=hb, h=h, tc=tc, ig=invg_done):
                slot = slot_box[0] % 4
                slot_box[0] += 1
                sf = sc_free[slot]

                def _t2_mm(tns, slot=slot, sf=sf, ig=ig):
                    if sf[0] is not None:
                        _w(tns, sf[0], sf[1])
                    _w(tns, s_dv, ig)
                    tns.matmul(sc_slot(slot), onesrow[0:1, :], rowgb[0:1, :],
                               start=True, stop=True).then_inc(s_mm.h, 1)
                DF.t(_t2_mm)
                s_mm.n += 1
                bcg_mm = s_mm.n

                base2 = s_dv.n

                def _t2_v(v, slot=slot, bcg_mm=bcg_mm, base=base2):
                    _w(v, s_mm, bcg_mm)
                    v.tensor_copy(bcg[:], sc_slot(slot)).then_inc(s_dv.h, 1)
                    v.wait_ge(s_dv.h, base + 1)
                    v.tensor_mul(cgn[0:64, :], P_cg[0:64, 0:512], bcg[0:64, :]).then_inc(s_dv.h, 1)
                DF.v(_t2_v)
                s_dv.n += 2
                sc_free[slot] = (s_dv, s_dv.n - 1)
                frees["cg"] = s_dv.n
                tail_state["cgn"] = s_dv.n

            def t3(h=h, tc=tc):
                def _t3_mm(tns, cd=tail_state["cgn"], gf=frees["G"]):
                    _w(tns, s_dv, max(cd, gf))
                    tns.matmul(P_G[:, 0:512], uoTb[:], cgn[0:64, :],
                               start=True, stop=True).then_inc(s_mm.h, 1)
                DF.t(_t3_mm)
                s_mm.n += 1
                G_mm = s_mm.n

                def _t3_v(v, h=h, tc=tc, G_mm=G_mm, cm=tail_state["ctmp"]):
                    _w(v, s_mm, G_mm)
                    v.tensor_add(ctx_all[:, h * TLOC + tc * 512:h * TLOC + (tc + 1) * 512],
                                 ctmp[:], P_G[:, 0:512]).then_inc(s_dv.h, 1)
                DF.v(_t3_v)
                s_dv.n += 1
                frees["G"] = s_dv.n

            tail_q.append(t1)
            tail_q.append(t2)
            tail_q.append(t3)

    for fn in tail_q:
        fn()
    tail_q.clear()
    emit_precast(len(pw_cols))
    DF.flush()

    if debug:
        @block.gpsimd
        def _(g):
            _w(g, s_dv, s_dv.n)
            g.dma_start(dbg["ctx"][:], ctx_all[:]).then_inc(s_st.h, 16)
        s_st.n += 16

    barrier()
    es2.close()
    es2p.close()

    if phases <= 2:
        return _finish(nc, blk, block, es, (s_ld, s_st, s_mm, s_dv, s_ac)), dbg

    es3 = ExitStack()
    wobf = [es3.enter_context(nc.sbuf_tensor(f"wobf{i}", [128, NCT * 128], BF)) for i in range(2)]
    aoT = es3.enter_context(nc.sbuf_tensor("aoT", [128, NCT * 512], F32))
    xtile = [es3.enter_context(nc.sbuf_tensor(f"p3x{i}", [128, C], F32)) for i in range(2)]
    x1b = es3.enter_context(nc.sbuf_tensor("x1b", [128, C], F32))
    xn2b = es3.enter_context(nc.sbuf_tensor("xn2b", [128, C], BF))
    st3 = es3.enter_context(nc.sbuf_tensor("st3", [128, 24], F32))
    tmp3 = es3.enter_context(nc.sbuf_tensor("tmp3", [128, 16], F32))
    xn2st = es3.enter_context(nc.sbuf_tensor("xn2st", [128, NCT * 512], BF))
    es3p = ExitStack()
    p3_po = [es3p.enter_context(nc.psum_tensor(f"p3po{i}", [128, 512], F32)) for i in range(2)]
    p3_tr = es3p.enter_context(nc.psum_tensor("p3tr", [128, 2048], F32))
    p3_trb = es3p.enter_context(nc.psum_tensor("p3trb", [128, 2048], BF))

    po_free = [0, 0]
    wobf_free = [0, 0]
    tr_free = 0
    trb_free = 0
    xtile_free = [0, 0]
    x1b_free = 0
    xn2b_free = 0

    for tc in range(4):
        for co in range(NCT):
            pb = co % 2
            wb3 = co % 2

            @block.sync
            def _(sync, co=co, wb3=wb3, wf=wobf_free[wb3]):
                _w(sync, s_mm, wf)
                sync.dma_start(wobf[wb3][:], scr_wobf[co]).then_inc(s_ld.h, 16)
            s_ld.n += 16
            wld3 = s_ld.n

            @block.tensor
            def _(tns, tc=tc, co=co, pb=pb, wb3=wb3, pf=po_free[pb], wc=wld3):
                _w(tns, s_ld, wc)
                _w(tns, s_dv, pf)
                for kt in range(NCT):
                    i = tns.matmul(p3_po[pb][:], wobf[wb3][:, kt * 128:(kt + 1) * 128],
                                   ctx_all[:, kt * TLOC + tc * 512:kt * TLOC + tc * 512 + 512],
                                   start=(kt == 0), stop=(kt == NCT - 1))
                i.then_inc(s_mm.h, 1)
            s_mm.n += 1
            po_done = s_mm.n
            wobf_free[wb3] = po_done

            @block.vector
            def _(v, co=co, pb=pb):
                _w(v, s_mm, po_done)
                v.tensor_copy(aoT[:, co * 512:(co + 1) * 512], p3_po[pb][:]).then_inc(s_dv.h, 1)
            s_dv.n += 1
            po_free[pb] = s_dv.n
        ao_done = s_dv.n

        for tti in range(4):
            xb = tti % 2
            trow = tc * 512 + tti * 128

            @block.sync
            def _(sync, xb=xb, trow=trow, xf=xtile_free[xb]):
                _w(sync, s_dv, xf)
                sync.dma_start(xtile[xb][:], dx[trow:trow + 128, :]).then_inc(s_ld.h, 16)
            s_ld.n += 16
            x_ld = s_ld.n

            @block.tensor
            def _(tns, tti=tti, tf=tr_free):
                _w(tns, s_dv, max(ao_done, tf))
                for co in range(NCT):
                    i = tns.transpose(p3_tr[:, co * 128:(co + 1) * 128],
                                      aoT[:, co * 512 + tti * 128:co * 512 + tti * 128 + 128],
                                      identF[:])
                i.then_inc(s_mm.h, 1)
            s_mm.n += 1
            tr_done = s_mm.n

            add_cnt = s_dv.n + 1

            @block.vector
            def _(v, xb=xb, x1f=x1b_free, ac=add_cnt):
                _w(v, s_mm, tr_done)
                _w(v, s_ld, x_ld)
                _w(v, s_st, x1f)
                v.tensor_add(x1b[:], p3_tr[:], xtile[xb][:]).then_inc(s_dv.h, 1)
                v.wait_ge(s_dv.h, ac)
                _ln_stats(v, x1b, st3, tmp3, vchain).then_inc(s_dv.h, 1)
            s_dv.n += 2
            tr_free = s_dv.n
            xtile_free[xb] = s_dv.n
            var3_done = s_dv.n

            @block.scalar
            def _(s):
                _w(s, s_dv, var3_done)
                s.activation(tmp3[:, 7:8], tmp3[:, 5:6], AF.Sqrt, bias=CST_EPS).then_inc(s_ac.h, 1)
            s_ac.n += 1
            sd3_done = s_ac.n

            @block.vector
            def _(v, xnf=xn2b_free):
                _w(v, s_ac, sd3_done)
                _w(v, s_mm, xnf)
                vchain(v.reciprocal(tmp3[:, 6:7], tmp3[:, 7:8]))
                v.tensor_scalar(xn2b[:], x1b[:], tmp3[:, 2:3], tmp3[:, 6:7],
                                ALU.subtract, ALU.mult).then_inc(s_dv.h, 1)
            s_dv.n += 1
            xn2_done = s_dv.n

            @block.gpsimd
            def _(g, trow=trow):
                _w(g, s_dv, xn2_done)
                g.dma_start(scr_x1[trow:trow + 128, :], x1b[:]).then_inc(s_st.h, 16)
                if debug:
                    g.dma_start(dbg["x1"][trow:trow + 128, :], x1b[:]).then_inc(s_st.h, 16)
            s_st.n += 32 if debug else 16
            x1_st = s_st.n
            x1b_free = x1_st

            @block.tensor
            def _(tns, tbf=trb_free):
                _w(tns, s_dv, max(xn2_done, tbf))
                for ct in range(NCT):
                    i = tns.transpose(p3_trb[:, ct * 128:(ct + 1) * 128],
                                      xn2b[:, ct * 128:(ct + 1) * 128], identB[:])
                i.then_inc(s_mm.h, 1)
            s_mm.n += 1
            trb_done = s_mm.n
            xn2b_free = trb_done

            @block.vector
            def _(v, tti=tti):
                _w(v, s_mm, trb_done)
                for ct in range(NCT):
                    i = v.tensor_copy(xn2st[:, ct * 512 + tti * 128:ct * 512 + tti * 128 + 128],
                                      p3_trb[:, ct * 128:(ct + 1) * 128])
                i.then_inc(s_dv.h, 1)
            s_dv.n += 1
            trb_free = s_dv.n
        chunk_evac_done = s_dv.n

        @block.gpsimd
        def _(g, tc=tc):
            _w(g, s_dv, chunk_evac_done)
            g.dma_start(scr_xn2[tc].rearrange("ct p t -> p ct t"),
                        xn2st.ap().rearrange("p (ct t) -> p ct t", ct=NCT)).then_inc(s_st.h, 16)
            if debug:
                g.dma_start(dbg["xn2"][tc].rearrange("ct p t -> p ct t"),
                            xn2st.ap().rearrange("p (ct t) -> p ct t", ct=NCT)).then_inc(s_st.h, 16)
        s_st.n += 32 if debug else 16
        xn2st_free = s_st.n

        @block.vector
        def _(v, f=xn2st_free):
            _w(v, s_st, f)
            v.engine_nop().then_inc(s_dv.h, 1)
        s_dv.n += 1
        trb_free = s_dv.n

    barrier()
    es3.close()
    es3p.close()
    es_c.close()

    if phases <= 3:
        return _finish(nc, blk, block, es, (s_ld, s_st, s_mm, s_dv, s_ac)), dbg

    es4 = ExitStack()
    sgb = es4.enter_context(nc.sbuf_tensor("sgb", [128, NHT], F32))
    sub = es4.enter_context(nc.sbuf_tensor("sub", [128, NHT], F32))
    sdb = es4.enter_context(nc.sbuf_tensor("sdb", [128, NCT], F32))
    xnf = es4.enter_context(nc.sbuf_tensor("xnf", [128, NCT * 1024], BF))
    wgbf = [es4.enter_context(nc.sbuf_tensor(f"wgbf{i}", [128, NCT * 128], BF)) for i in range(2)]
    wubf = [es4.enter_context(nc.sbuf_tensor(f"wubf{i}", [128, NCT * 128], BF)) for i in range(2)]
    silu_sb = [es4.enter_context(nc.sbuf_tensor(f"silu{i}", [128, 512], F32)) for i in range(2)]
    hstage = [es4.enter_context(nc.sbuf_tensor(f"hstage{i}", [128, 1024], BF)) for i in range(2)]
    hchunk = es4.enter_context(nc.sbuf_tensor("hchunk", [128, NHT * 512], BF))
    wdbf = [es4.enter_context(nc.sbuf_tensor(f"wdbf{i}", [128, NHT * 128], BF)) for i in range(2)]
    dst_sb = [es4.enter_context(nc.sbuf_tensor(f"dst{i}", [128, 512], BF)) for i in range(2)]
    asm_sb = [es4.enter_context(nc.sbuf_tensor(f"asm{i}", [128, NCT * 128], BF)) for i in range(2)]
    x1t = [es4.enter_context(nc.sbuf_tensor(f"x1t{i}", [128, C], F32)) for i in range(2)]
    x2b = es4.enter_context(nc.sbuf_tensor("x2b", [128, C], F32))
    es4p = ExitStack()
    p4_g = [es4p.enter_context(nc.psum_tensor(f"p4g{i}", [128, 512], F32)) for i in range(2)]
    p4_u = [es4p.enter_context(nc.psum_tensor(f"p4u{i}", [128, 512], F32)) for i in range(2)]
    p4_d = [es4p.enter_context(nc.psum_tensor(f"p4d{i}", [128, 512], F32)) for i in range(2)]
    p4_trb = es4p.enter_context(nc.psum_tensor("p4trb", [128, 2048], BF))

    DF4 = Defer(block)

    @block.sync
    def _(sync):
        sync.dma_start(sgb[:], dsg[:]).then_inc(s_ld.h, 16)
        sync.dma_start(sub[:], dsu[:]).then_inc(s_ld.h, 16)
        sync.dma_start(sdb[:], dsd[:]).then_inc(s_ld.h, 16)
    s_ld.n += 48
    sc_ld = s_ld.n

    wgbf_free = [0, 0]
    g_free = [0, 0]
    u_free = [0, 0]
    silu_free = [0, 0]
    hstage_free = [0, 0]
    xnf_free = 0
    hchunk_ld_prev = 0
    wdbf_free = [0, 0]
    d_free = [0, 0]
    dst_free = [0, 0]
    asm_free = [0, 0]
    scr_d_rd_prev = 0
    x1t_free = [0, 0]
    x2b_free = 0

    for tb in range(2):
        def _df4_y_1(sync, tb=tb, xf=xnf_free):
            if tb == 0:
                _w(sync, s_ld, sc_ld)
            _w(sync, s_mm, xf)
            for sch in range(2):
                sync.dma_start(
                    xnf.ap().rearrange("p (ct s t) -> p ct s t", ct=NCT, s=2)[:, :, sch, :],
                    scr_xn2[2 * tb + sch].rearrange("ct p t -> p ct t")).then_inc(s_ld.h, 16)
        DF4.y(_df4_y_1)
        s_ld.n += 32
        xnf_ld = s_ld.n

        for ht in range(NHT):
            wb = ht % 2

            def _df4_y_2(sync, ht=ht, wb=wb, bf=wgbf_free[wb]):
                _w(sync, s_mm, bf)
                sync.dma_start(wgbf[wb][:], scr_wgbf[ht]).then_inc(s_ld.h, 16)
                sync.dma_start(wubf[wb][:], scr_wubf[ht]).then_inc(s_ld.h, 16)
            DF4.y(_df4_y_2)
            s_ld.n += 32
            wcast = s_ld.n

            for tc2 in range(2):
                pb = tc2

                def _df4_t_6(tns, wb=wb, tc2=tc2, pb=pb, gf=g_free[pb], uf=u_free[pb], xl=xnf_ld, wcast=wcast):
                    if tc2 == 0:
                        _w(tns, s_ld, max(wcast, xl))
                    _w(tns, s_ac, gf)
                    for kt in range(NCT):
                        tns.matmul(p4_g[pb][:], wgbf[wb][:, kt * 128:(kt + 1) * 128],
                                   xnf[:, kt * 1024 + tc2 * 512:kt * 1024 + tc2 * 512 + 512],
                                   start=(kt == 0), stop=(kt == NCT - 1))
                    _w(tns, s_dv, uf)
                    for kt in range(NCT):
                        i = tns.matmul(p4_u[pb][:], wubf[wb][:, kt * 128:(kt + 1) * 128],
                                       xnf[:, kt * 1024 + tc2 * 512:kt * 1024 + tc2 * 512 + 512],
                                       start=(kt == 0), stop=(kt == NCT - 1))
                    i.then_inc(s_mm.h, 1)
                DF4.t(_df4_t_6)
                s_mm.n += 1
                gu_done = s_mm.n
                if tc2 == 1:
                    wgbf_free[wb] = gu_done

                def _df4_s_9(s, ht=ht, pb=pb, sf=silu_free[pb], gu_done=gu_done):
                    _w(s, s_mm, gu_done)
                    _w(s, s_dv, sf)
                    s.activation(silu_sb[pb][:], p4_g[pb][:], AF.Silu,
                                 scale=sgb[:, ht:ht + 1]).then_inc(s_ac.h, 1)
                DF4.s(_df4_s_9)
                s_ac.n += 1
                g_free[pb] = s_ac.n
                silu_done = s_ac.n

                def _df4_v_11(v, ht=ht, tc2=tc2, pb=pb, wb=wb, hf=hstage_free[wb], silu_done=silu_done):
                    _w(v, s_ac, silu_done)
                    if tc2 == 0:
                        _w(v, s_st, hf)
                    v.scalar_tensor_tensor(hstage[wb][:, tc2 * 512:(tc2 + 1) * 512],
                                           p4_u[pb][:], sub[:, ht:ht + 1], silu_sb[pb][:],
                                           ALU.mult, ALU.mult).then_inc(s_dv.h, 1)
                DF4.v(_df4_v_11)
                s_dv.n += 1
                u_free[pb] = s_dv.n
                silu_free[pb] = s_dv.n
            hst_done = s_dv.n

            def _df4_g_15(g, ht=ht, wb=wb, hp=hchunk_ld_prev, tb=tb, hst_done=hst_done):
                _w(g, s_dv, hst_done)
                if tb == 1 and ht < 2:
                    _w(g, s_ld, hp)
                g.dma_start(scr_h[ht], hstage[wb][:]).then_inc(s_st.h, 16)
                if debug:
                    g.dma_start(dbg["h"][tb][ht], hstage[wb][:]).then_inc(s_st.h, 16)
            DF4.g(_df4_g_15)
            s_st.n += 32 if debug else 16
            hstage_free[wb] = s_st.n
        hstore_done = s_st.n
        xnf_free = s_mm.n

        for tc2 in range(2):
            def _df4_y_3(sync, tc2=tc2, hs=hstore_done, sdp=scr_d_rd_prev, tb=tb,
                  mmg=s_mm.n):
                _w(sync, s_st, hs)
                _w(sync, s_mm, mmg)
                if tc2 == 0 and tb == 1:
                    _w(sync, s_ld, sdp)
                sync.dma_start(hchunk.ap().rearrange("p (ht t) -> p ht t", ht=NHT),
                               scr_h.rearrange("ht p t -> p ht t")[:, :, tc2 * 512:(tc2 + 1) * 512]
                               ).then_inc(s_ld.h, 16)
            DF4.y(_df4_y_3)
            s_ld.n += 16
            hchunk_ld = s_ld.n
            if tc2 == 1:
                hchunk_ld_prev = hchunk_ld

            for co in range(NCT):
                wdb = co % 2

                def _df4_y_4(sync, co=co, wdb=wdb, wf=wdbf_free[wdb]):
                    _w(sync, s_mm, wf)
                    sync.dma_start(wdbf[wdb][:], scr_wdbf[co]).then_inc(s_ld.h, 16)
                DF4.y(_df4_y_4)
                s_ld.n += 16
                wd_ld = s_ld.n

                pb = co % 2

                def _df4_t_7(tns, co=co, pb=pb, wdb=wdb, df=d_free[pb], hl=hchunk_ld, wd_ld=wd_ld):
                    _w(tns, s_ld, max(wd_ld, hl if co == 0 else 0))
                    _w(tns, s_dv, df)
                    for kt in range(NHT):
                        i = tns.matmul(p4_d[pb][:], wdbf[wdb][:, kt * 128:(kt + 1) * 128],
                                       hchunk[:, kt * 512:(kt + 1) * 512],
                                       start=(kt == 0), stop=(kt == NHT - 1))
                    i.then_inc(s_mm.h, 1)
                DF4.t(_df4_t_7)
                s_mm.n += 1
                d_done = s_mm.n
                wdbf_free[wdb] = d_done

                db = co % 2

                def _df4_v_13(v, co=co, pb=pb, db=db, dsf=dst_free[db], d_done=d_done):
                    _w(v, s_mm, d_done)
                    _w(v, s_st, dsf)
                    v.tensor_scalar(dst_sb[db][:], p4_d[pb][:], sdb[:, co:co + 1],
                                    None, ALU.mult).then_inc(s_dv.h, 1)
                DF4.v(_df4_v_13)
                s_dv.n += 1
                d_free[pb] = s_dv.n
                devac_done = s_dv.n

                def _df4_g_16(g, co=co, tc2=tc2, db=db, tb=tb, devac_done=devac_done):
                    _w(g, s_dv, devac_done)
                    g.dma_start(scr_d[co][:, tc2 * 512:(tc2 + 1) * 512], dst_sb[db][:]).then_inc(s_st.h, 16)
                    if debug:
                        g.dma_start(dbg["dn"][tb][co][:, tc2 * 512:(tc2 + 1) * 512],
                                    dst_sb[db][:]).then_inc(s_st.h, 16)
                DF4.g(_df4_g_16)
                s_st.n += 32 if debug else 16
                dst_free[db] = s_st.n
        dstore_done = s_st.n

        trb4_free = 0
        for tti in range(8):
            xb = tti % 2
            trow = tb * 1024 + tti * 128

            def _df4_y_5(sync, xb=xb, tti=tti, trow=trow, xf=x1t_free[xb], af=asm_free[xb], ds=dstore_done):
                _w(sync, s_dv, xf)
                sync.dma_start(x1t[xb][:], scr_x1[trow:trow + 128, :]).then_inc(s_ld.h, 16)
                _w(sync, s_mm, af)
                _w(sync, s_st, ds)
                sync.dma_start(asm_sb[xb].ap().rearrange("p (co t) -> p co t", co=NCT),
                               scr_d.rearrange("co p t -> p co t")[:, :, (trow % 1024):(trow % 1024) + 128]
                               ).then_inc(s_ld.h, 16)
            DF4.y(_df4_y_5)
            s_ld.n += 32
            x1_ld = s_ld.n
            if tti == 7:
                scr_d_rd_prev = x1_ld

            def _df4_t_8(tns, xb=xb, tf=trb4_free, xl=x1_ld):
                _w(tns, s_ld, xl)
                _w(tns, s_dv, tf)
                for co in range(NCT):
                    i = tns.transpose(p4_trb[:, co * 128:(co + 1) * 128],
                                      asm_sb[xb][:, co * 128:(co + 1) * 128],
                                      identB[:])
                i.then_inc(s_mm.h, 1)
            DF4.t(_df4_t_8)
            s_mm.n += 1
            tr4_done = s_mm.n
            asm_free[xb] = tr4_done

            def _df4_v_14(v, xb=xb, xf=x2b_free, tr4_done=tr4_done):
                _w(v, s_mm, tr4_done)
                _w(v, s_st, xf)
                v.tensor_add(x2b[:], p4_trb[:], x1t[xb][:]).then_inc(s_dv.h, 1)
            DF4.v(_df4_v_14)
            s_dv.n += 1
            trb4_free = s_dv.n
            x1t_free[xb] = s_dv.n
            x2_done = s_dv.n

            def _df4_g_17(g, trow=trow, x2_done=x2_done):
                _w(g, s_dv, x2_done)
                g.dma_start(dout[trow:trow + 128, :], x2b[:]).then_inc(s_st.h, 16)
            DF4.g(_df4_g_17)
            s_st.n += 16
            x2b_free = s_st.n

        DF4.flush()

    return _finish(nc, blk, block, es, (s_ld, s_st, s_mm, s_dv, s_ac)), dbg


def _finish(nc, blk, block, es, sems):
    s_ld, s_st, s_mm, s_dv, s_ac = sems

    @block.sync
    def _(sync):
        for s in sems:
            if s.n > 0:
                sync.wait_ge(s.h, s.n)
    blk.__exit__(None, None, None)
    es.close()
    return nc


def _host_prep(inputs):
    x = np.asarray(inputs["L0_attn_x"], dtype=np.float32)
    w_qkv = np.asarray(inputs["L0_attn_w_qkv"], dtype=np.float32)
    w_o = np.asarray(inputs["L0_attn_w_o"], dtype=np.float32)
    pk = np.asarray(inputs["L0_attn_pk"], dtype=np.float32)
    pv = np.asarray(inputs["L0_attn_pv"], dtype=np.float32)
    uo = np.asarray(inputs["L0_attn_uo"], dtype=np.float32)
    w1 = np.asarray(inputs["L0_attn_info_w1"], dtype=np.float32)
    w2 = np.asarray(inputs["L0_attn_info_w2"], dtype=np.float32)
    wg = np.asarray(inputs["L0_ffn_wq_gate"], dtype=np.int32)
    sg = np.asarray(inputs["L0_ffn_s_gate"], dtype=np.float32)
    wu = np.asarray(inputs["L0_ffn_wq_up"], dtype=np.int32)
    su = np.asarray(inputs["L0_ffn_s_up"], dtype=np.float32)
    wd = np.asarray(inputs["L0_ffn_wq_down"], dtype=np.int32)
    sd = np.asarray(inputs["L0_ffn_s_down"], dtype=np.float32)

    shared = dict(
        wqkvT=np.ascontiguousarray(w_qkv.T),
        woT=np.ascontiguousarray(w_o.T),
        pkT=np.ascontiguousarray(pk.T),
        pvT=np.ascontiguousarray(pv.T),
        uoT=np.ascontiguousarray(uo.T),
        w1T=np.ascontiguousarray(w1.T),
        w2T=np.ascontiguousarray(w2.reshape(4, 128).T),
        wgT=np.ascontiguousarray(wg.T),
        wuT=np.ascontiguousarray(wu.T),
        wdT=np.ascontiguousarray(wd.T),
        sg2=np.ascontiguousarray(sg.reshape(NHT, 128).T),
        su2=np.ascontiguousarray(su.reshape(NHT, 128).T),
        sd2=np.ascontiguousarray(sd.reshape(NCT, 128).T),
        identB=np.eye(128, dtype=np.float32).astype(BF16NP),
        identF=np.eye(128, dtype=np.float32),
    )

    inv = 1.0 / (10000.0 ** (np.arange(0, D, 2, dtype=np.float64) / D))
    tt = np.arange(4096, dtype=np.float64)
    freqs = np.outer(tt, inv)
    emb = np.concatenate([freqs, freqs], axis=1)
    cosT = np.cos(emb).T.astype(np.float32)
    sinT = np.sin(emb).T.astype(np.float32)
    sign = np.where(np.arange(128) % 2 == 0, -1.0, 1.0)[:, None].astype(np.float32)
    sinT = sinT * sign

    sl = np.arange(128)[:, None]
    tl = np.arange(128)[None, :]
    masklo = (tl >= sl).astype(np.float32).astype(BF16NP)
    maskhi = (tl < sl).astype(np.float32).astype(BF16NP)

    cstcol = np.zeros((128, 8), dtype=np.float32)
    cstcol[:, 0] = SHIFT_L
    cstcol[:, 1] = SHIFT_G
    cstcol[:, 2] = EPS
    cstcol[:, 3] = 1.0
    onesrow = np.ones((1, 128), dtype=np.float32).astype(BF16NP)

    in_maps = []
    for core in range(8):
        b = core // 2
        half = core % 2
        T0 = half * TLOC
        P0 = (1 - half) * TLOC
        x_own = np.ascontiguousarray(x[b, T0:T0 + TLOC, :])
        x_par = np.ascontiguousarray(x[b, P0:P0 + TLOC, :])
        order = np.concatenate([np.arange(T0, T0 + TLOC), np.arange(P0, P0 + TLOC)])
        ropeC = np.ascontiguousarray(cosT[:, order])
        ropeS = np.ascontiguousarray(sinT[:, order])
        bm = np.zeros((4, 128, 512), dtype=np.float32)
        if half == 1:
            for j in range(4):
                off = 512 - j * 128
                tcols = np.arange(512)[None, :]
                srows = np.arange(128)[:, None]
                rel = tcols + off - srows
                bm[j] = ((rel >= 0) & (rel <= 511)).astype(np.float32)
        bmask = np.ascontiguousarray(bm.astype(BF16NP))

        m = dict(x_own=x_own, x_par=x_par,
                 ropeC=ropeC, ropeS=ropeS, bmask=bmask,
                 masklo=masklo, maskhi=maskhi, cst=cstcol, onesrow=onesrow,
                 **shared)
        in_maps.append(m)
    return in_maps


def kernel(**inputs):
    in_maps = _host_prep(inputs)
    nc, _ = build(debug=False, phases=5)
    res = run_bass_kernel_spmd(nc, in_maps, core_ids=list(range(8)))
    out = np.empty((B, T, C), dtype=np.float32)
    for core in range(8):
        b = core // 2
        T0 = (core % 2) * TLOC
        out[b, T0:T0 + TLOC, :] = res.results[core]["out"]
    return out
